# version 07
# baseline (speedup 1.0000x reference)
import numpy as np
import ml_dtypes
import concourse.bass as bass
import concourse.mybir as mybir
from concourse.bass_utils import run_bass_kernel_spmd
from contextlib import ExitStack

F32 = mybir.dt.float32
BF16 = mybir.dt.bfloat16
AF = mybir.ActivationFunctionType
ALU = mybir.AluOpType

D, S, H, HD, DH = 1024, 2048, 16, 64, 4096
NCH, NHC = D // 128, DH // 128
CG, NCG, NT = 512, 4, 16
SHIFT, MASKV, EPS, INV_D = 12.0, -1.0e9, 1e-5, 1.0 / 1024
PSLOT, RING = 2048, 5
BF = ml_dtypes.bfloat16
A_STAGE = "full"
A_SUB = "full"
SIM_EDGES = False


class Sem:
    __slots__ = ("name", "planned", "handle", "eng")

    def __init__(self, name, eng=None):
        self.name, self.planned, self.handle, self.eng = name, 0, None, eng

    def mark(self):
        return (self, self.planned)


class Sched:
    ENGS = ("sync", "gpsimd", "tensor", "vector", "scalar")

    def __init__(self):
        self.events = []

    def ev(self, eng, fn, waits=(), incs=(), tag=""):
        best = {}
        for wv in waits:
            if wv is None:
                continue
            s, v = wv
            if v <= 0:
                continue
            if v > best.get(s, 0):
                best[s] = v
        w = list(best.items())
        self.events.append((eng, fn, w, list(incs), tag))
        for s, n in incs:
            s.planned += n

    def simulate(self):
        from collections import deque
        qs = {e: deque() for e in self.ENGS}
        for e in self.events:
            qs[e[0]].append(e)
        counts = {}
        progress = True
        while progress:
            progress = False
            for eng in self.ENGS:
                q = qs[eng]
                while q:
                    _, _, w, incs, _ = q[0]
                    if all(counts.get(s, 0) >= v for s, v in w):
                        for s, n in incs:
                            counts[s] = counts.get(s, 0) + n
                        q.popleft()
                        progress = True
                    else:
                        break
        if any(qs.values()):
            lines = []
            for e, q in qs.items():
                if q:
                    _, _, w, _, tag = q[0]
                    miss = [(s.name, v, counts.get(s, 0)) for s, v in w
                            if counts.get(s, 0) < v]
                    lines.append(f"  {e}: {len(q)} stuck; first='{tag}' missing={miss}")
            raise RuntimeError("schedule deadlock:\n" + "\n".join(lines))


def _f32c(a):
    return np.ascontiguousarray(np.asarray(a, dtype=np.float32))


def host_layout(inputs):
    g = {}
    x = _f32c(inputs["L0_attn_a_x"])
    w_qkv = _f32c(inputs["L0_attn_a_w_qkv"])
    b_qkv = _f32c(inputs["L0_attn_a_b_qkv"])
    g["wa_qk"] = np.ascontiguousarray(w_qkv[:, :2 * D])
    g["wa_v"] = np.ascontiguousarray(w_qkv[:, 2 * D:])
    g["ba_q"] = np.ascontiguousarray(b_qkv[:D].reshape(NCH, 128).T)
    g["ba_k"] = np.ascontiguousarray(b_qkv[D:2 * D].reshape(NCH, 128).T)
    g["ba_v"] = np.ascontiguousarray(b_qkv[2 * D:].reshape(1, D))
    g["wa_out"] = _f32c(inputs["L0_attn_a_w_out"])
    g["ba_out"] = np.ascontiguousarray(
        _f32c(inputs["L0_attn_a_b_out"]).reshape(1, D))
    for nm, key in (("wb_q", "L0_attn_b_WQ"), ("wb_k", "L0_attn_b_WK"),
                    ("wb_v", "L0_attn_b_WV")):
        g[nm] = np.ascontiguousarray(_f32c(inputs[key]).transpose(2, 0, 1).reshape(D, D))
    g["wb_o"] = np.ascontiguousarray(
        _f32c(inputs["L0_attn_b_WO"]).transpose(0, 2, 1).reshape(D, D))
    g["w1t"] = np.ascontiguousarray(_f32c(inputs["L0_ffn_w1_w"]).T)
    g["w2t"] = np.ascontiguousarray(_f32c(inputs["L0_ffn_w2_w"]).T)
    g["h1t"] = np.ascontiguousarray(_f32c(inputs["L0_ffn_h1_w"]).T)
    g["h2t"] = np.ascontiguousarray(_f32c(inputs["L0_ffn_h2_w"]).T)
    g["w3t"] = np.ascontiguousarray(_f32c(inputs["L0_ffn_w3_w"]).T)
    g["s1"] = np.ascontiguousarray(_f32c(inputs["L0_ffn_w1_s"]).reshape(NHC, 128).T)
    g["s2"] = np.ascontiguousarray(_f32c(inputs["L0_ffn_w2_s"]).reshape(NHC, 128).T)
    g["sh1"] = np.ascontiguousarray(_f32c(inputs["L0_ffn_h1_s"]).reshape(NHC, 128).T)
    g["sh2"] = np.ascontiguousarray(_f32c(inputs["L0_ffn_h2_s"]).reshape(NHC, 128).T)
    g["s3"] = np.ascontiguousarray(_f32c(inputs["L0_ffn_w3_s"]).reshape(NCH, 128).T)

    row = np.arange(128)[:, None].astype(np.float32)
    col = np.arange(128)[None, :].astype(np.float32)
    g["mask_diag"] = np.where(np.abs(col - row) <= 64, 0.0, MASKV).astype(BF)
    g["mask_B"] = np.where(row < 64,
                           np.where(row <= col - 64, 0.0, MASKV),
                           np.where(row >= col + 64, 0.0, MASKV)).astype(BF)
    ub = np.arange(896)[None, :].astype(np.float32)
    g["mask_big"] = np.where(ub - 384 >= row, 0.0, MASKV).astype(BF)
    g["ident"] = np.eye(128, dtype=np.float32).astype(BF)
    g["ones_row"] = np.ones((1, 128), np.float32)
    g["ones_rb"] = np.ones((1, 512), np.float32).astype(BF)
    g["ones_col"] = np.ones((128, 1), np.float32)
    per_core = [{"xT": np.ascontiguousarray(x[b].T)} for b in range(x.shape[0])]
    return g, per_core


def build_program(n_sub=4, do_prep=True):
    nc = bass.Bass("TRN2", target_bir_lowering=False)
    sch = Sched()

    def din(name, shape, dtype=F32):
        return nc.dram_tensor(name, list(shape), dtype, kind="ExternalInput").ap()

    xT = din("xT", [D, S])
    IN = {}
    for nm, shp, dt in [
            ("wa_qk", [D, 2 * D], F32), ("wa_v", [D, D], F32),
            ("ba_q", [128, NCH], F32), ("ba_k", [128, NCH], F32),
            ("ba_v", [1, D], F32), ("wa_out", [D, D], F32), ("ba_out", [1, D], F32),
            ("wb_q", [D, D], F32), ("wb_k", [D, D], F32), ("wb_v", [D, D], F32),
            ("wb_o", [D, D], F32),
            ("w1t", [D, DH], F32), ("w2t", [D, DH], F32), ("h1t", [DH, DH], F32),
            ("h2t", [DH, DH], F32), ("w3t", [DH, D], F32),
            ("s1", [128, NHC], F32), ("s2", [128, NHC], F32), ("sh1", [128, NHC], F32),
            ("sh2", [128, NHC], F32), ("s3", [128, NCH], F32),
            ("mask_diag", [128, 128], BF16), ("mask_B", [128, 128], BF16),
            ("mask_big", [128, 896], BF16), ("ident", [128, 128], BF16),
            ("ones_row", [1, 128], F32), ("ones_rb", [1, 512], BF16),
            ("ones_col", [128, 1], F32)]:
        IN[nm] = din(nm, shp, dt)
    yT = nc.dram_tensor("yT", [D, S], F32, kind="ExternalOutput").ap()

    ctx = ExitStack()

    def sb(name, shape, dtype):
        return ctx.enter_context(nc.sbuf_tensor(name, shape, dtype))

    x_fm = sb("x_fm", [128, NCH * S], F32)
    xln = sb("xln", [128, NCH * CG], BF16)
    big = sb("big", [128, 16384 + 16 * 16 * 66 + 4096 + 4096], BF16)
    OFF_K, OFF_V = 0, 16384
    OFF_Q = OFF_V + 16 * 16 * 66
    OFF_STK = OFF_Q + 4096
    VSLOT = 66
    VSTR = H * VSLOT
    wbuf = sb("wbuf", [128, RING * PSLOT], BF16)
    rows_a = sb("rows_a", [1, 2 * CG], F32)
    rows_b = sb("rows_b", [1, 2 * CG], F32)
    rows1 = sb("rows1", [1, 1024], F32)
    sqb = sb("sqb", [128, CG], F32)
    t1 = sb("t1", [128, CG], F32)
    fscr = sb("fscr", [128, CG], F32)
    srow = sb("srow", [128, CG], F32)
    bcs = sb("bcs", [128, CG], F32)
    pbuf = sb("pbuf", [128, 2 * CG], BF16)
    m_diag = sb("m_diag", [128, 128], BF16)
    m_B = sb("m_B", [128, 128], BF16)
    m_big = sb("m_big", [128, 896], BF16)
    ident_s = sb("ident_s", [128, 128], BF16)
    onesr = sb("onesr", [1, 128], F32)
    onesrb = sb("onesrb", [1, 512], BF16)
    onesc = sb("onesc", [128, 1], F32)
    onescb = sb("onescb", [128, 1], BF16)
    xbt = sb("xbt", [128, CG], BF16)
    ones64 = sb("ones64", [128, 64], F32)
    bq_s = sb("bq_s", [128, NCH], F32)
    bk_s = sb("bk_s", [128, NCH], F32)
    bks2 = sb("bks2", [128, NCH], F32)
    bav_s = sb("bav_s", [1, D], BF16)
    bao_s = sb("bao_s", [1, D], BF16)
    s1_s = sb("s1_s", [128, NHC], F32)
    s2_s = sb("s2_s", [128, NHC], F32)
    sh1_s = sb("sh1_s", [128, NHC], F32)
    sh2_s = sb("sh2_s", [128, NHC], F32)
    s3_s = sb("s3_s", [128, NCH], F32)
    c12 = sb("c12", [128, 1], F32)
    ceps = sb("ceps", [128, 1], F32)
    assert nc.sbuf_bytes_remaining >= 0, f"SBUF over: {nc.sbuf_bytes_remaining}"

    PSB = [ctx.enter_context(nc.psum_tensor(f"psb{i}", [128, 512], F32))
           for i in range(8)]
    bank_reader = [None] * 8
    bank_writer = [None] * 8

    sdX, sdM, sdO, sdB = Sem("dX"), Sem("dM"), Sem("dO"), Sem("dB")
    sdW = [Sem(f"dW{i}") for i in range(RING)]
    sdWc = [Sem(f"dWc{i}") for i in range(RING)]
    sdK = [Sem(f"dK{i}") for i in range(RING)]
    sT, sV, sC = Sem("T", "tensor"), Sem("V", "vector"), Sem("C", "scalar")
    ALLSEMS = [sdX, sdM, sdO, sdB] + sdW + sdWc + sdK + [sT, sV, sC]

    def dma(eng, s, dst, src, waits=(), tag=""):
        sch.ev(eng, lambda e, d=dst, r=src: e.dma_start(d, r),
               waits=waits, incs=[(s, 16)], tag=tag)
        return s.mark()

    def vec(fn, waits=(), tag=""):
        sch.ev("vector", fn, waits=waits, incs=[(sV, 1)], tag=tag)
        return sV.mark()

    def sca(fn, waits=(), tag=""):
        sch.ev("scalar", fn, waits=waits, incs=[(sC, 1)], tag=tag)
        return sC.mark()

    class Accum:

        def __init__(self, bank, tag=""):
            self.bank, self.first, self.tag = bank, True, tag
            self.started = set()

        def add(self, o, l, r, waits=(), last=False, inc=False, region="d",
                stop=None):
            w = list(waits)
            if self.first and bank_reader[self.bank] is not None:
                w.append(bank_reader[self.bank])
            self.first = False
            st = region not in self.started
            self.started.add(region)
            sp = last if stop is None else stop
            sch.ev("tensor",
                   lambda e, o=o, l=l, r=r, st=st, sp=sp:
                       e.matmul(o, l, r, start=st, stop=sp),
                   waits=w, incs=[(sT, 1)] if (last or inc) else [], tag=self.tag)
            if last:
                bank_writer[self.bank] = sT.mark()
                return bank_writer[self.bank]
            return sT.mark() if inc else None

    def mm_group(bank, mms, tag=""):
        acc = Accum(bank, tag)
        mk = None
        for i, (o, l, r, w) in enumerate(mms):
            mk = acc.add(o, l, r, waits=w, last=(i == len(mms) - 1))
        return mk

    def evac_done(bank, mark):
        bank_reader[bank] = mark

    class Plan:

        def __init__(self, name, src, din_, dout_, kind):
            self.name, self.kind = name, kind
            self.kch = din_ // 128
            csz = 128 if kind == "lhsT" else 512
            self.csz = csz
            self.nmaj = dout_ // csz
            self.epm = self.kch * csz
            total = self.nmaj * self.epm
            assert total % PSLOT == 0
            self.nblk = total // PSLOT
            if self.epm <= PSLOT:
                assert PSLOT % self.epm == 0
                self.mper = PSLOT // self.epm
                self.kpb = self.kch
            else:
                assert self.epm % PSLOT == 0
                self.mper = 0
                self.kpb = PSLOT // csz
            self.t = nc.dram_tensor("pk_" + name, [self.nblk, 128, PSLOT], BF16,
                                    kind="Internal").ap()
            self.blk_mark = [None] * self.nblk
            self.view = src.rearrange("(k p) m -> p k m", p=128)

        def block_src(self, b):
            if self.mper:
                m0 = b * self.mper
                return self.view[:, :, m0 * self.csz:(m0 + self.mper) * self.csz]
            nbpm = self.kch // self.kpb
            maj, kb = b // nbpm, b % nbpm
            return self.view[:, kb * self.kpb:(kb + 1) * self.kpb,
                             maj * self.csz:(maj + 1) * self.csz]

        def tile_loc(self, maj, k):
            if self.mper:
                blk = maj // self.mper
                off = k * self.mper * self.csz + (maj % self.mper) * self.csz
                return blk, off
            nbpm = self.kch // self.kpb
            return maj * nbpm + k // self.kpb, (k % self.kpb) * self.csz

    PAQK = Plan("aqk", IN["wa_qk"], D, 2 * D, "lhsT")
    PAV = Plan("av", IN["wa_v"], D, D, "rhs")
    PAOUT = Plan("aout", IN["wa_out"], D, D, "lhsT")
    PW1 = Plan("w1", IN["w1t"], D, DH, "lhsT")
    PW2 = Plan("w2", IN["w2t"], D, DH, "lhsT")
    PH1 = Plan("h1", IN["h1t"], DH, DH, "lhsT")
    PH2 = Plan("h2", IN["h2t"], DH, DH, "lhsT")
    PW3 = Plan("w3", IN["w3t"], DH, D, "lhsT")
    PBK = Plan("bk", IN["wb_k"], D, D, "lhsT")
    PBV = Plan("bv", IN["wb_v"], D, D, "rhs")
    PBQ = Plan("bq", IN["wb_q"], D, D, "lhsT")
    PBO = Plan("bo", IN["wb_o"], D, D, "lhsT")

    for ch in range(NCH):
        dma("sync", sdX, x_fm[:, ch * S:(ch + 1) * S],
            xT[ch * 128:(ch + 1) * 128, :], tag=f"ldx{ch}")
    mk_x = sdX.mark()
    for dst, key in [(m_diag, "mask_diag"), (m_B, "mask_B"), (m_big, "mask_big"),
                     (ident_s, "ident"), (onesr, "ones_row"), (onesrb, "ones_rb"),
                     (onesc, "ones_col"), (bq_s, "ba_q"), (bk_s, "ba_k"),
                     (s1_s, "s1"), (s2_s, "s2"),
                     (sh1_s, "sh1"), (sh2_s, "sh2"), (s3_s, "s3")]:
        dma("sync", sdM, dst[:], IN[key], tag="ldmisc")
    mk_misc = sdM.mark()
    vec(lambda e: e.memset(c12[:], -SHIFT), tag="c12")
    vec(lambda e: e.memset(ceps[:], EPS), tag="ceps")
    vec(lambda e: e.memset(ones64[:], 1.0), tag="ones64")
    vec(lambda e: e.tensor_scalar(bks2[:], bk_s[:], 0.125, None, ALU.mult),
        waits=[mk_misc], tag="bks2")
    mk_consts = vec(lambda e: e.tensor_copy(onescb[:], onesc[:]),
                    waits=[mk_misc], tag="onescb")
    mk_bav = dma("gpsimd", sdB, bav_s[:], IN["ba_v"], tag="bavc")
    mk_bao = dma("gpsimd", sdB, bao_s[:], IN["ba_out"], tag="baoc")


    stream = {"i": 0, "freed": []}

    class WCursor:

        def __init__(self, plan, tag):
            self.plan, self.tag = plan, tag
            self.blk, self.sbase, self.lmk = None, None, None
            self.load_idx = None
            self.pending = []
            self.last_mark = None
            self.pack_pending = None

        def _depart(self):
            idx = len(stream["freed"])
            assert idx == self.load_idx, (idx, self.load_idx, self.tag)
            stream["freed"].append(None)
            self.pending.append((idx, self.pack_pending))

        def tile(self, maj, k):
            b, off = self.plan.tile_loc(maj, k)
            if b != self.blk:
                if self.blk is not None:
                    self._depart()
                i = stream["i"]
                stream["i"] += 1
                slot = i % RING
                w = []
                if i >= RING:
                    fm = stream["freed"][i - RING]
                    assert fm is not None, f"ring free unresolved at load {i}"
                    w += fm
                slot_ap = wbuf[:, slot * PSLOT:(slot + 1) * PSLOT]
                if self.plan.blk_mark[b] is None:
                    self.lmk = dma("gpsimd", sdWc[slot], slot_ap,
                                   self.plan.block_src(b), waits=w,
                                   tag=f"wc:{self.plan.name}:{b}")
                    pk_mk = dma("sync", sdK[slot], self.plan.t[b], slot_ap,
                                waits=[self.lmk], tag=f"wp:{self.plan.name}:{b}")
                    self.plan.blk_mark[b] = pk_mk
                    self.pack_pending = pk_mk
                else:
                    eng = "sync" if slot % 2 == 0 else "gpsimd"
                    self.lmk = dma(eng, sdW[slot], slot_ap, self.plan.t[b],
                                   waits=w + [self.plan.blk_mark[b]],
                                   tag=f"wl:{self.plan.name}:{b}")
                    self.pack_pending = None
                self.blk, self.sbase, self.load_idx = b, slot * PSLOT, i
            return (wbuf[:, self.sbase + off:self.sbase + off + self.plan.csz],
                    self.lmk)

        def used(self):
            self.last_mark = sT.mark()
            for idx, pk in self.pending:
                stream["freed"][idx] = [self.last_mark] + ([pk] if pk else [])
            self.pending = []

        def close(self):
            if self.blk is not None:
                self.used()
                idx = len(stream["freed"])
                assert idx == self.load_idx
                stream["freed"].append([self.last_mark] +
                                       ([self.pack_pending] if self.pack_pending
                                        else []))
                self.blk = None

    hz = {"xln_rd": None, "sq_rd": [None, None], "rows_rd": None,
          "stk_rd": None, "q_rd": None, "pb_rd": [None, None],
          "fscr_C": None, "fscr_V": None,
          "srow_rd": [None, None], "bcs_rd": [None, None]}
    res = {}
    vev = {}

    def ln_stats(c, sub, tag):
        rw = [res.get((sub - 1, c))] if (sub - 1, c) in res else []
        accA = Accum(6, tag + ":sx")
        accB = Accum(7, tag + ":sxx")
        xslots = [xbt[:], fscr[:].bitcast(BF16)[:, 0:CG]]
        sslots = [sqb[:].bitcast(BF16)[:, 0:CG], t1[:].bitcast(BF16)[:, 0:CG]]
        mkA_s = [hz["sq_rd"][0], hz["fscr_V"]]
        mkB_s = [hz["sq_rd"][1], hz["sq_rd"][1]]
        mkA = mkB = None
        for ch in range(NCH):
            sl = ch % 2
            xsl = x_fm[:, ch * S + c * CG: ch * S + (c + 1) * CG]
            w = (list(rw) + [mk_x]) if ch == 0 else []
            if mkA_s[sl] is not None:
                w.append(mkA_s[sl])
            mk_c = vec(lambda e, o=xslots[sl], a=xsl: e.tensor_copy(o, a),
                       waits=w, tag=tag + f":xc{ch}")
            w2 = [mk_c]
            if mkB_s[sl] is not None:
                w2.append(mkB_s[sl])
            smk = vec(lambda e, o=sslots[sl], a=xslots[sl]:
                      e.tensor_tensor(o, a, a, ALU.mult),
                      waits=w2, tag=tag + f":sq{ch}")
            wA = [mk_c] + ([mk_consts] if ch == 0 else [])
            mkA = accA.add(PSB[6][0:1, :], onescb[:, 0:1], xslots[sl], waits=wA,
                           last=(ch == NCH - 1), inc=True)
            mkA_s[sl] = mkA
            mkB = accB.add(PSB[7][0:1, :], onescb[:, 0:1], sslots[sl], waits=[smk],
                           last=(ch == NCH - 1), inc=True)
            mkB_s[sl] = mkB
        hz["sq_rd"][0] = mkA
        hz["sq_rd"][1] = mkB
        hz["fscr_V"] = mkA
        rm, rv = rows1[0:1, 0:CG], rows1[0:1, CG:2 * CG]
        msq = srow[0:1, 0:CG]
        mk = vec(lambda e: e.tensor_scalar(rm, PSB[6][0:1, :], INV_D, None, ALU.mult),
                 waits=[mkA], tag=tag + ":m")
        mk = vec(lambda e: e.tensor_tensor(msq, rm, rm, ALU.mult), waits=[mk],
                 tag=tag + ":msq")
        mk = vec(lambda e: e.scalar_tensor_tensor(rv, PSB[7][0:1, :], INV_D, msq,
                                                  ALU.mult, ALU.subtract),
                 waits=[mk, mkB], tag=tag + ":var")
        evac_done(6, mk)
        evac_done(7, mk)
        mk = sca(lambda e: e.activation(msq, rv, AF.Ln, bias=ceps[0:1, 0:1]),
                 waits=[mk, mk_consts], tag=tag + ":ln")
        aslc = rows_a[0:1, (c % 2) * CG:(c % 2 + 1) * CG]
        bslc = rows_b[0:1, (c % 2) * CG:(c % 2 + 1) * CG]
        w = [mk] + ([hz["rows_rd"]] if hz["rows_rd"] else [])
        mka = sca(lambda e: e.activation(aslc, msq, AF.Exp, scale=-0.5),
                  waits=w, tag=tag + ":a")
        mkb = vec(lambda e: e.scalar_tensor_tensor(bslc, rm, -1.0, aslc,
                                                   ALU.mult, ALU.mult),
                  waits=[mka], tag=tag + ":b")
        return [mka, mkb]

    def ln_apply(c, row_marks, tag):
        aslc = rows_a[0:1, (c % 2) * CG:(c % 2 + 1) * CG]
        bslc = rows_b[0:1, (c % 2) * CG:(c % 2 + 1) * CG]
        mkA = mm_group(6, [(PSB[6][:, :], onesr[0:1, 0:128], aslc,
                            list(row_marks) + [mk_misc])], tag + ":bca")
        mkB = mm_group(7, [(PSB[7][:, :], onesr[0:1, 0:128], bslc,
                            list(row_marks) + [mk_misc])], tag + ":bcb")
        hz["rows_rd"] = sT.mark()
        last = None
        for ch in range(NCH):
            xsl = x_fm[:, ch * S + c * CG: ch * S + (c + 1) * CG]
            xo = xln[:, ch * CG:(ch + 1) * CG]
            w = [mkA]
            if last is not None:
                w.append(last)
            if ch == 0 and hz["xln_rd"]:
                w.append(hz["xln_rd"])
            if ch == 0 and hz["fscr_C"]:
                w.append(hz["fscr_C"])
            mk1 = vec(lambda e, a=xsl: e.tensor_tensor(fscr[:], a, PSB[6][:, :], ALU.mult),
                      waits=w, tag=tag + f":ap{ch}")
            last = vec(lambda e, o=xo: e.tensor_tensor(o, fscr[:], PSB[7][:, :], ALU.add),
                       waits=[mk1, mkB], tag=tag + f":ab{ch}")
        hz["fscr_V"] = last
        evac_done(6, last)
        evac_done(7, last)
        return [last]

    def proj_xln(plan, mlist, xm, evac, tag, banks=(0, 1)):
        cur = WCursor(plan, tag)
        for mi, m in enumerate(mlist):
            bank = banks[mi % 2]
            mms = []
            for k in range(plan.kch):
                lh, lmk = cur.tile(m, k)
                w = [lmk]
                if k == 0:
                    w += list(xm)
                mms.append((PSB[bank][:, :], lh, xln[:, k * CG:(k + 1) * CG], w))
            mkw = mm_group(bank, mms, tag + f":m{m}")
            cur.used()
            mk = evac(mi, m, bank, mkw)
            evac_done(bank, mk)
        cur.close()
        hz["xln_rd"] = sT.mark()

    def v_gen(c, plan, with_bias, xm, tag):
        cur = WCursor(plan, tag)
        for t4 in range(4):
            tt_ = c * 4 + t4
            for n in range(2):
                bank = 2 + (t4 * 2 + n) % 2
                mms = []
                for k in range(NCH):
                    rh, lmk = cur.tile(n, k)
                    lh = xln[:, k * CG + t4 * 128: k * CG + t4 * 128 + 128]
                    w = [lmk]
                    if k == 0:
                        w += list(xm)
                    mms.append((PSB[bank][:, :], lh, rh, w))
                if with_bias:
                    mms.append((PSB[bank][:, :], onesrb[0:1, 0:128],
                                bav_s[0:1, n * 512:(n + 1) * 512], [mk_bao]))
                mkw = mm_group(bank, mms, tag + f":V{t4}.{n}")
                cur.used()
                lastmk = None
                for hh in range(8):
                    h = n * 8 + hh
                    dst = big[:, OFF_V + tt_ * VSTR + h * VSLOT:
                              OFF_V + tt_ * VSTR + h * VSLOT + 64]
                    w = [mkw] if hh == 0 else []
                    lastmk = vec(lambda e, d=dst,
                                 s=PSB[bank][:, hh * 64:(hh + 1) * 64]:
                                 e.tensor_copy(d, s), waits=w,
                                 tag=tag + f":Vev{t4}.{n}.{hh}")
                vev[(tt_, n)] = lastmk
                evac_done(bank, lastmk)
        cur.close()
        hz["xln_rd"] = sT.mark()

    def attn_proj(plan, sub, c, bias, stk_mark, tag):
        cur = WCursor(plan, tag)
        last_res = None
        for m in range(NCH):
            bank = m % 2
            mms = []
            for kp in range(NCH):
                lh, lmk = cur.tile(m, kp)
                stk = big[:, OFF_STK + kp * CG: OFF_STK + (kp + 1) * CG]
                w = [lmk]
                if kp == 0:
                    w.append(stk_mark)
                mms.append((PSB[bank][:, :], lh, stk, w))
            if bias:
                mms.append((PSB[bank][:, :], bao_s[0:1, m * 128:(m + 1) * 128],
                            onesrb[0:1, 0:512], [mk_bao]))
            mkw = mm_group(bank, mms, tag + f":m{m}")
            cur.used()
            xsl = x_fm[:, m * S + c * CG: m * S + (c + 1) * CG]
            last_res = vec(lambda e, o=xsl, ps=PSB[bank][:, :]:
                           e.tensor_tensor(o, o, ps, ALU.add),
                           waits=[mkw], tag=tag + f":res{m}")
            evac_done(bank, last_res)
        cur.close()
        hz["stk_rd"] = sT.mark()
        res[(sub, c)] = last_res

    def norm_stack(h, t4, width, b_o, pv_mk, tag):
        par = h % 2
        rrow = 64 if par == 0 else 32
        b_bc = 6 + par
        w = [pv_mk]
        if hz["srow_rd"][par]:
            w.append(hz["srow_rd"][par])
        mk_r = vec(lambda e: e.reciprocal(srow[rrow:rrow + 1, 0:width],
                                          PSB[b_o][rrow:rrow + 1, 0:width]),
                   waits=w, tag=tag + ":rcp")
        mk_bc = mm_group(b_bc, [((PSB[b_bc][0:64, 0:width] if par == 0
                                  else PSB[b_bc][64:128, 0:width]),
                                 ones64[rrow:rrow + 1, 0:64],
                                 srow[rrow:rrow + 1, 0:width], [mk_r, mk_consts])],
                        tag + ":bc")
        hz["srow_rd"][par] = sT.mark()
        rb = 0 if par == 0 else 64
        w = [mk_bc]
        if hz["bcs_rd"][par]:
            w.append(hz["bcs_rd"][par])
        mk_cp = sca(lambda e: e.copy(bcs[rb:rb + 64, 0:width],
                                     PSB[b_bc][rb:rb + 64, 0:width]),
                    waits=w, tag=tag + ":cpy")
        evac_done(b_bc, mk_cp)
        hp = h // 2
        if width == 128:
            dst = big[:, OFF_STK + hp * CG + t4 * 128:
                      OFF_STK + hp * CG + t4 * 128 + 128]
        else:
            dst = big[:, OFF_STK + hp * CG: OFF_STK + (hp + 1) * CG]
        dst = dst[rb:rb + 64, :]
        vrows = PSB[b_o][0:64, 0:width] if par == 0 else PSB[b_o][64:128, 0:width]
        w = [mk_cp]
        if hz["stk_rd"]:
            w.append(hz["stk_rd"])
        mk_n = vec(lambda e: e.tensor_tensor(dst, vrows, bcs[rb:rb + 64, 0:width],
                                             ALU.mult),
                   waits=w, tag=tag + ":mul")
        hz["bcs_rd"][par] = mk_n
        evac_done(b_o, mk_n)
        return mk_n

    def attn_a(sub, arena_entry):
        mk_vones = vec(lambda e: e.memset(big[:, OFF_V:OFF_V + NT * VSTR], 1.0),
                       waits=[mk_consts] + list(arena_entry), tag="A:vones")
        mk_pz = mk_vones
        for sl in range(2):
            mk_pz = vec(lambda e, a=pbuf[0:64, sl * CG + 128: sl * CG + 256]:
                        e.memset(a, 0.0), tag="A:pz")
            mk_pz = vec(lambda e, a=pbuf[64:128, sl * CG + 256: sl * CG + 384]:
                        e.memset(a, 0.0), tag="A:pz")
        qmk = {}
        p0v = {}

        rowmk = {}

        def part0(c):
            rmk = ln_stats(c, sub, f"A:ln{c}")
            rowmk[c] = rmk
            xm = ln_apply(c, rmk, f"A:lap{c}")
            if A_STAGE == "ln":
                return

            def kev(mi, m, bank, mkw):
                pr = m - 8
                dst = big[:, OFF_K + pr * S + c * CG: OFF_K + pr * S + (c + 1) * CG]
                w = [mkw, mk_consts] + list(arena_entry)
                return vec(lambda e, d=dst, b=bank: e.tensor_scalar(
                    d, PSB[b][:, :], 0.125, bks2[:, pr:pr + 1], ALU.mult, ALU.add),
                    waits=w, tag=f"A:Kev{c}.{pr}")
            proj_xln(PAQK, list(range(8, 16)), xm, kev, f"A:K{c}")
            v_gen(c, PAV, True, list(xm) + [mk_vones], f"A:V{c}")
            return

        def part1(c):
            if A_STAGE in ("ln", "kv"):
                return
            xm = ln_apply(c, rowmk[c], f"A:lapq{c}")

            def qev(mi, m, bank, mkw):
                dst = big[:, OFF_Q + m * CG: OFF_Q + (m + 1) * CG]
                w = [mkw, mk_misc]
                if hz["q_rd"]:
                    w.append(hz["q_rd"])
                mk = vec(lambda e, d=dst, b=bank, m=m: e.tensor_scalar(
                    d, PSB[b][:, :], bq_s[:, m:m + 1], None, ALU.add),
                    waits=w, tag=f"A:Qev{c}.{m}")
                qmk[m] = mk
                return mk
            proj_xln(PAQK, list(range(8)), xm, qev, f"A:Q{c}")
            if A_STAGE == "q":
                return
            qlast = qmk[7]
            stk_mark = None
            items = [(c * 4 + t4, t4, h) for t4 in range(4) for h in range(H)]
            s1 = s2 = None
            for it in items + [None, None]:
                st = (a_scores(c, it[0], it[1], it[2], qlast,
                               f"A:t{it[0]}h{it[2]}") if it is not None else None)
                pvst = a_pv(s1) if s1 is not None else None
                if s2 is not None:
                    stk_mark = norm_stack(s2["h"], s2["t4"], 128, s2["b_o"],
                                          s2["pv_mk"], s2["tag"])
                s1, s2 = st, pvst
            hz["q_rd"] = sT.mark()
            if A_STAGE == "scores":
                return
            attn_proj(PAOUT, sub, c, True, stk_mark, f"A:pj{c}")

        def a_scores(c, t, t4, h, qlast, tag):
            hp, hr = h // 2, 64 * (h % 2)
            idx = t * H + h
            b_d, b_B = idx % 2, 2 + idx % 2
            kbase = OFF_K + hp * S
            qsl = big[:, OFF_Q + hp * CG + t4 * 128: OFF_Q + hp * CG + t4 * 128 + 128]
            qsl = qsl[hr:hr + 64, :]
            lh = big[:, kbase + t * 128: kbase + t * 128 + 128][hr:hr + 64, :]
            kcov = p0v[min(c + 1, NCG - 1)]
            mkd = mm_group(b_d, [
                (PSB[b_d][:, 0:128], lh, qsl, [qlast, kcov]),
                (PSB[b_d][:, 0:128], ident_s[:, :], m_diag[:, :], [mk_misc])],
                tag + ":d")
            accB = Accum(b_B, tag + ":B")
            nparts = int(t < NT - 1) + int(t > 0)
            np_i = 0
            if t < NT - 1:
                lh2 = big[:, kbase + (t + 1) * 128: kbase + (t + 1) * 128 + 64]
                accB.add(PSB[b_B][0:64, 0:128], lh2[hr:hr + 64, :], qsl, region="n")
                np_i += 1
                accB.add(PSB[b_B][0:64, 0:128], ident_s[:, 0:64], m_B[:, :],
                         waits=[mk_misc], region="n", last=(np_i == nparts),
                         stop=True)
            if t > 0:
                lh3 = big[:, kbase + t * 128 - 64: kbase + t * 128]
                accB.add(PSB[b_B][64:128, 0:128], lh3[hr:hr + 64, :], qsl, region="p")
                np_i += 1
                accB.add(PSB[b_B][64:128, 0:128], ident_s[:, 64:128], m_B[:, :],
                         waits=[mk_misc], region="p", last=(np_i == nparts),
                         stop=True)
            mkB = bank_writer[b_B]
            ps = idx % 2
            p_d = pbuf[:, ps * CG: ps * CG + 128]
            p_prev = pbuf[:, ps * CG + 128: ps * CG + 256]
            p_next = pbuf[:, ps * CG + 256: ps * CG + 384]
            w = [mkd, mk_pz] + ([hz["pb_rd"][ps]] if hz["pb_rd"][ps] else [])
            mk_pd = sca(lambda e, o=p_d: e.activation(o, PSB[b_d][:, 0:128], AF.Exp,
                                                      bias=c12[:, 0:1]),
                        waits=w + [mk_consts], tag=tag + ":ed")
            evac_done(b_d, mk_pd)
            mk_pB = mk_pd
            if t > 0:
                mk_pB = sca(lambda e, o=p_prev[64:128, :], a=PSB[b_B][64:128, 0:128],
                            bb=c12[64:128, 0:1]:
                            e.activation(o, a, AF.Exp, bias=bb),
                            waits=[mkB], tag=tag + ":eBp")
            if t < NT - 1:
                mk_pB = sca(lambda e, o=p_next[0:64, :], a=PSB[b_B][0:64, 0:128],
                            bb=c12[0:64, 0:1]:
                            e.activation(o, a, AF.Exp, bias=bb),
                            waits=[mkB], tag=tag + ":eBn")
            evac_done(b_B, mk_pB)
            return dict(t=t, t4=t4, h=h, ps=ps, p_d=p_d, p_prev=p_prev,
                        p_next=p_next, mk_pd=mk_pd, mk_pB=mk_pB, tag=tag)

        def a_pv(st):
            t, t4, h, ps = st["t"], st["t4"], st["h"], st["ps"]
            b_o = 4 + (t * H + h) % 2

            def vs(j, a, bb):
                return big[:, OFF_V + j * VSTR + h * VSLOT + a:
                           OFF_V + j * VSTR + h * VSLOT + bb]
            vw = [vev[(t, h // 8)], st["mk_pd"], st["mk_pB"]]
            if t > 0:
                vw.append(vev[(t - 1, h // 8)])
            if t < NT - 1:
                vw.append(vev[(t + 1, h // 8)])
            acc = Accum(b_o, st["tag"] + ":pv")
            p_d, p_prev, p_next = st["p_d"], st["p_prev"], st["p_next"]
            mms = []
            if h % 2 == 0:
                mms.append((PSB[b_o][0:65, 0:128], vs(t, 0, 65), p_d, vw, "v"))
                if t > 0:
                    mms.append((PSB[b_o][0:65, 0:128], vs(t - 1, 0, 65), p_prev,
                                [], "v"))
                if t < NT - 1:
                    mms.append((PSB[b_o][0:65, 0:128], vs(t + 1, 0, 65), p_next,
                                [], "v"))
            else:
                mms.append((PSB[b_o][64:128, 0:128], vs(t, 0, 64), p_d, vw, "v"))
                mms.append((PSB[b_o][32:33, 0:128], vs(t, 64, 65), p_d, [], "s"))
                if t > 0:
                    mms.append((PSB[b_o][64:128, 0:128], vs(t - 1, 0, 64), p_prev,
                                [], "v"))
                    mms.append((PSB[b_o][32:33, 0:128], vs(t - 1, 64, 65), p_prev,
                                [], "s"))
                if t < NT - 1:
                    mms.append((PSB[b_o][64:128, 0:128], vs(t + 1, 0, 64), p_next,
                                [], "v"))
                    mms.append((PSB[b_o][32:33, 0:128], vs(t + 1, 64, 65), p_next,
                                [], "s"))
            last_of_region = {}
            for i, (_, _, _, _, rg) in enumerate(mms):
                last_of_region[rg] = i
            pv_mk = None
            for i, (o, l, r, w, rg) in enumerate(mms):
                pv_mk = acc.add(o, l, r, waits=w, region=rg,
                                last=(i == len(mms) - 1),
                                stop=(last_of_region[rg] == i))
            hz["pb_rd"][ps] = sT.mark()
            return dict(h=h, t4=t4, b_o=b_o, pv_mk=pv_mk, tag=st["tag"])

        order = [("p0", 0), ("p0", 1), ("p1", 0), ("p0", 2), ("p1", 1),
                 ("p0", 3), ("p1", 2), ("p1", 3)]
        for kind, c in order:
            if kind == "p0":
                part0(c)
                p0v[c] = sV.mark()
            else:
                part1(c)
        return sT.mark()

    def ffn(sub, arena_entry):
        gmk = [None] * NHC
        hmk = [None] * NHC
        g2mk = [None] * NHC
        for c in range(NCG):
            rmk = ln_stats(c, sub, f"F{sub}:ln{c}")
            xm = ln_apply(c, rmk, f"F{sub}:lap{c}")
            cur1 = WCursor(PW1, f"F{sub}:w1.{c}")
            cur2 = WCursor(PW2, f"F{sub}:w2.{c}")
            for m in range(NHC):
                bA, bB = m % 2, 2 + m % 2
                mmsA = []
                mmsB = []
                for k in range(NCH):
                    lh1, l1 = cur1.tile(m, k)
                    w = [l1]
                    if k == 0:
                        w += list(xm)
                    mmsA.append((PSB[bA][:, :], lh1, xln[:, k * CG:(k + 1) * CG], w))
                mkA = mm_group(bA, mmsA, f"F{sub}:w1.{c}.{m}")
                cur1.used()
                for k in range(NCH):
                    lh2, l2 = cur2.tile(m, k)
                    mmsB.append((PSB[bB][:, :], lh2, xln[:, k * CG:(k + 1) * CG],
                                 [l2]))
                mkB = mm_group(bB, mmsB, f"F{sub}:w2.{c}.{m}")
                cur2.used()
                w = [mkA, mk_misc] + ([hz["fscr_V"]] if hz["fscr_V"] else [])
                mk_s = sca(lambda e, b=bA, m=m: e.activation(
                    fscr[:], PSB[b][:, :], AF.Silu, scale=s1_s[:, m:m + 1]),
                    waits=w, tag=f"F{sub}:silu{c}.{m}")
                hz["fscr_C"] = mk_s
                evac_done(bA, mk_s)
                dst = big[:, m * CG:(m + 1) * CG]
                w = [mk_s, mkB]
                if m == 0 and c == 0:
                    w += list(arena_entry)
                mk_g = vec(lambda e, d=dst, b=bB, m=m: e.scalar_tensor_tensor(
                    d, fscr[:], s2_s[:, m:m + 1], PSB[b][:, :], ALU.mult, ALU.mult),
                    waits=w, tag=f"F{sub}:g{c}.{m}")
                hz["fscr_V"] = mk_g
                evac_done(bB, mk_g)
                gmk[m] = mk_g
            cur1.close()
            cur2.close()
            hz["xln_rd"] = sT.mark()
            cur = WCursor(PH1, f"F{sub}:h1.{c}")
            for m in range(NHC):
                bank = m % 2
                mms = []
                for k in range(NHC):
                    lh, lmk = cur.tile(m, k)
                    w = [lmk]
                    if k == 0 and m == 0:
                        w.append(gmk[NHC - 1])
                        w += list(arena_entry)
                    mms.append((PSB[bank][:, :], lh, big[:, k * CG:(k + 1) * CG], w))
                mkw = mm_group(bank, mms, f"F{sub}:h1.{c}.{m}")
                cur.used()
                dst = big[:, OFF_V + m * CG: OFF_V + (m + 1) * CG]
                hmk[m] = sca(lambda e, d=dst, b=bank, m=m: e.activation(
                    d, PSB[b][:, :], AF.Gelu, scale=sh1_s[:, m:m + 1]),
                    waits=[mkw, mk_misc], tag=f"F{sub}:gel1.{c}.{m}")
                evac_done(bank, hmk[m])
            cur.close()
            mk_h1_T = sT.mark()
            cur = WCursor(PH2, f"F{sub}:h2.{c}")
            for m in range(NHC):
                bank = m % 2
                mms = []
                for k in range(NHC):
                    lh, lmk = cur.tile(m, k)
                    w = [lmk]
                    if k == 0 and m == 0:
                        w.append(hmk[NHC - 1])
                    mms.append((PSB[bank][:, :], lh,
                                big[:, OFF_V + k * CG: OFF_V + (k + 1) * CG], w))
                mkw = mm_group(bank, mms, f"F{sub}:h2.{c}.{m}")
                cur.used()
                dst = big[:, m * CG:(m + 1) * CG]
                w = [mkw, mk_misc]
                if m == 0:
                    w.append(mk_h1_T)
                g2mk[m] = sca(lambda e, d=dst, b=bank, m=m: e.activation(
                    d, PSB[b][:, :], AF.Gelu, scale=sh2_s[:, m:m + 1]),
                    waits=w, tag=f"F{sub}:gel2.{c}.{m}")
                evac_done(bank, g2mk[m])
            cur.close()
            cur = WCursor(PW3, f"F{sub}:w3.{c}")
            last_res = None
            for m in range(NCH):
                bank = m % 2
                mms = []
                for k in range(NHC):
                    lh, lmk = cur.tile(m, k)
                    w = [lmk]
                    if k == 0 and m == 0:
                        w.append(g2mk[NHC - 1])
                    mms.append((PSB[bank][:, :], lh, big[:, k * CG:(k + 1) * CG], w))
                mkw = mm_group(bank, mms, f"F{sub}:w3.{c}.{m}")
                cur.used()
                xsl = x_fm[:, m * S + c * CG: m * S + (c + 1) * CG]
                last_res = vec(lambda e, o=xsl, b=bank, m=m: e.scalar_tensor_tensor(
                    o, PSB[b][:, :], s3_s[:, m:m + 1], o, ALU.mult, ALU.add),
                    waits=[mkw, mk_misc], tag=f"F{sub}:res{c}.{m}")
                evac_done(bank, last_res)
            cur.close()
            res[(sub, c)] = last_res
        return sT.mark()

    def attn_b(sub, arena_entry):
        mk_vones = vec(lambda e: e.memset(big[:, OFF_V:OFF_V + NT * VSTR], 1.0),
                       waits=[mk_consts] + list(arena_entry), tag="B:vones")
        for c in range(NCG):
            rmk = ln_stats(c, sub, f"B:ln{c}")
            xm = ln_apply(c, rmk, f"B:lap{c}")

            def kev(mi, m, bank, mkw):
                dst = big[:, OFF_K + m * S + c * CG: OFF_K + m * S + (c + 1) * CG]
                w = [mkw] + list(arena_entry)
                return sca(lambda e, d=dst, b=bank: e.activation(
                    d, PSB[b][:, :], AF.Copy, scale=0.125), waits=w,
                    tag=f"B:Kev{c}.{m}")
            proj_xln(PBK, list(range(NCH)), xm, kev, f"B:K{c}")
            v_gen(c, PBV, False, list(xm) + [mk_vones], f"B:V{c}")
            xm = ln_apply(c, rmk, f"B:lapq{c}")

            def qev(mi, m, bank, mkw):
                dst = big[:, OFF_Q + m * CG: OFF_Q + (m + 1) * CG]
                w = [mkw]
                if hz["q_rd"]:
                    w.append(hz["q_rd"])
                return sca(lambda e, d=dst, b=bank: e.activation(
                    d, PSB[b][:, :], AF.Copy), waits=w, tag=f"B:Qev{c}.{m}")
            proj_xln(PBQ, list(range(NCH)), xm, qev, f"B:Q{c}")
            qlast = sC.mark()
            stk_mark = None
            pend = None
            for h in range(H):
                pv = b_head(c, h, qlast, f"B:c{c}h{h}")
                if pend is not None:
                    stk_mark = norm_stack(pend[0], None, 512, pend[1], pend[2],
                                          f"B:c{c}h{pend[0]}")
                pend = (h, 4 + h % 2, pv)
            stk_mark = norm_stack(pend[0], None, 512, pend[1], pend[2],
                                  f"B:c{c}h{pend[0]}")
            hz["q_rd"] = sT.mark()
            attn_proj(PBO, sub, c, False, stk_mark, f"B:pj{c}")
        return sT.mark()

    def b_head(c, h, qlast, tag):
        hp, hr = h // 2, 64 * (h % 2)
        b_o = 4 + h % 2
        qsl = big[:, OFF_Q + hp * CG: OFF_Q + (hp + 1) * CG][hr:hr + 64, :]
        nj = 4 * c + 4
        acc = Accum(b_o, tag + ":pv")
        pv_mk = None

        def scores(j):
            b_s = j % 2
            lh = big[:, OFF_K + hp * S + j * 128: OFF_K + hp * S + j * 128 + 128]
            mms = [(PSB[b_s][:, :], lh[hr:hr + 64, :], qsl, [qlast])]
            if j >= 4 * c:
                r = j - 4 * c
                mms.append((PSB[b_s][:, :], ident_s[:, :],
                            m_big[:, (3 - r) * 128:(3 - r) * 128 + 512], [mk_misc]))
            mk_s = mm_group(b_s, mms, tag + f":s{j}")
            ps = j % 2
            p = pbuf[:, ps * CG:(ps + 1) * CG]
            w = [mk_s] + ([hz["pb_rd"][ps]] if hz["pb_rd"][ps] else [])
            mk_p = sca(lambda e, o=p, a=PSB[b_s][:, :]: e.activation(
                o, a, AF.Exp, bias=c12[:, 0:1]),
                waits=w + [mk_consts], tag=tag + f":e{j}")
            evac_done(b_s, mk_p)
            return p, mk_p

        pend = scores(0)
        for j in range(nj):
            nxt = scores(j + 1) if j + 1 < nj else None
            p, mk_p = pend
            vsl = big[:, OFF_V + j * VSTR + h * VSLOT:
                      OFF_V + j * VSTR + h * VSLOT + 65]
            vw = [mk_p, vev[(j, h // 8)]]
            last = (j == nj - 1)
            if h % 2 == 0:
                pv_mk = acc.add(PSB[b_o][0:65, :], vsl, p, waits=vw, last=last,
                                region="v")
            else:
                acc.add(PSB[b_o][64:128, :], vsl[:, 0:64], p, waits=vw, region="v",
                        stop=last)
                pv_mk = acc.add(PSB[b_o][32:33, :], vsl[:, 64:65], p, last=last,
                                region="s")
            hz["pb_rd"][j % 2] = sT.mark()
            pend = nxt
        return pv_mk

    arena_mark = ()
    subfns = [(attn_a, 1), (ffn, 2), (attn_b, 3), (ffn, 4)]
    for fn, sub in subfns[:n_sub]:
        tm = fn(sub, arena_mark)
        arena_mark = (sT.mark(),)

    for ch in range(NCH):
        w = [res[(min(n_sub, 4), c)] for c in range(NCG) if (min(n_sub, 4), c) in res]
        if not w:
            w = [mk_x]
        dma("sync", sdO, yT[ch * 128:(ch + 1) * 128, :],
            x_fm[:, ch * S:(ch + 1) * S], waits=w, tag=f"out{ch}")
    sch.ev("sync", lambda e: e.nop(),
           waits=[(sdO, sdO.planned)]
                 + [(s, s.planned) for s in sdW + sdWc + sdK]
                 + [(sdB, sdB.planned)],
           tag="final")

    nc._tag_map = {}
    sch.simulate()
    with ctx:
        with ExitStack() as sctx:
            for s in ALLSEMS:
                s.handle = sctx.enter_context(nc.semaphore(s.name))
            with nc.Block() as block:
                def emit_for(eng_obj, eng_name):
                    wm = {}
                    for eng, fn, w, incs, tag in sch.events:
                        if eng != eng_name:
                            continue
                        for s, v in w:
                            if s.eng == eng_name and not SIM_EDGES:
                                continue
                            if wm.get(s, -1) < v:
                                eng_obj.wait_ge(s.handle, v)
                                wm[s] = v
                        instr = fn(eng_obj)
                        try:
                            nc._tag_map[instr.ins.name] = (eng_name, tag)
                        except Exception:
                            pass
                        for s, n in incs:
                            instr.then_inc(s.handle, n)

                block.sync(lambda e: emit_for(e, "sync"))
                block.gpsimd(lambda e: emit_for(e, "gpsimd"))
                block.tensor(lambda e: emit_for(e, "tensor"))
                block.vector(lambda e: emit_for(e, "vector"))
                block.scalar(lambda e: emit_for(e, "scalar"))
    return nc


_CACHE = {}


def kernel(**inputs):
    g, per_core = host_layout(inputs)
    if "nc" not in _CACHE:
        _CACHE["nc"] = build_program(4)
    nc = _CACHE["nc"]
    in_maps = []
    for pc in per_core:
        m = dict(g)
        m.update(pc)
        in_maps.append(m)
    r = run_bass_kernel_spmd(nc, in_maps, core_ids=list(range(8)))
    out = np.stack([np.ascontiguousarray(res["yT"].T) for res in r.results])
    return out.astype(np.float32)


# version 09
# speedup vs baseline: 1.2158x; 1.2158x over previous
import numpy as np
import ml_dtypes
import concourse.bass as bass
import concourse.mybir as mybir
from concourse.bass_utils import run_bass_kernel_spmd
from contextlib import ExitStack

F32 = mybir.dt.float32
BF16 = mybir.dt.bfloat16
AF = mybir.ActivationFunctionType
ALU = mybir.AluOpType

D, S, H, HD, DH = 1024, 2048, 16, 64, 4096
NCH, NHC = D // 128, DH // 128
CG, NCG, NT = 512, 4, 16
SHIFT, MASKV, EPS, INV_D = 12.0, -1.0e9, 1e-5, 1.0 / 1024
PSLOT, RING = 2048, 5
BF = ml_dtypes.bfloat16
A_STAGE = "full"
A_SUB = "full"
SIM_EDGES = False


class Sem:
    __slots__ = ("name", "planned", "handle", "eng")

    def __init__(self, name, eng=None):
        self.name, self.planned, self.handle, self.eng = name, 0, None, eng

    def mark(self):
        return (self, self.planned)


class Sched:
    ENGS = ("sync", "gpsimd", "tensor", "vector", "scalar")

    def __init__(self):
        self.events = []

    def ev(self, eng, fn, waits=(), incs=(), tag=""):
        best = {}
        for wv in waits:
            if wv is None:
                continue
            s, v = wv
            if v <= 0:
                continue
            if v > best.get(s, 0):
                best[s] = v
        w = list(best.items())
        self.events.append((eng, fn, w, list(incs), tag))
        for s, n in incs:
            s.planned += n

    def simulate(self):
        from collections import deque
        qs = {e: deque() for e in self.ENGS}
        for e in self.events:
            qs[e[0]].append(e)
        counts = {}
        progress = True
        while progress:
            progress = False
            for eng in self.ENGS:
                q = qs[eng]
                while q:
                    _, _, w, incs, _ = q[0]
                    if all(counts.get(s, 0) >= v for s, v in w):
                        for s, n in incs:
                            counts[s] = counts.get(s, 0) + n
                        q.popleft()
                        progress = True
                    else:
                        break
        if any(qs.values()):
            lines = []
            for e, q in qs.items():
                if q:
                    _, _, w, _, tag = q[0]
                    miss = [(s.name, v, counts.get(s, 0)) for s, v in w
                            if counts.get(s, 0) < v]
                    lines.append(f"  {e}: {len(q)} stuck; first='{tag}' missing={miss}")
            raise RuntimeError("schedule deadlock:\n" + "\n".join(lines))


def _f32c(a):
    return np.ascontiguousarray(np.asarray(a, dtype=np.float32))


def host_layout(inputs):
    g = {}
    x = _f32c(inputs["L0_attn_a_x"])
    w_qkv = _f32c(inputs["L0_attn_a_w_qkv"])
    b_qkv = _f32c(inputs["L0_attn_a_b_qkv"])
    g["wa_qk"] = np.ascontiguousarray(w_qkv[:, :2 * D])
    g["wa_v"] = np.ascontiguousarray(w_qkv[:, 2 * D:])
    g["ba_q"] = np.ascontiguousarray(b_qkv[:D].reshape(NCH, 128).T)
    g["ba_k"] = np.ascontiguousarray(b_qkv[D:2 * D].reshape(NCH, 128).T)
    g["ba_v"] = np.ascontiguousarray(b_qkv[2 * D:].reshape(1, D))
    g["wa_out"] = _f32c(inputs["L0_attn_a_w_out"])
    g["ba_out"] = np.ascontiguousarray(
        _f32c(inputs["L0_attn_a_b_out"]).reshape(1, D))
    for nm, key in (("wb_q", "L0_attn_b_WQ"), ("wb_k", "L0_attn_b_WK"),
                    ("wb_v", "L0_attn_b_WV")):
        g[nm] = np.ascontiguousarray(_f32c(inputs[key]).transpose(2, 0, 1).reshape(D, D))
    g["wb_o"] = np.ascontiguousarray(
        _f32c(inputs["L0_attn_b_WO"]).transpose(0, 2, 1).reshape(D, D))
    g["w1t"] = np.ascontiguousarray(_f32c(inputs["L0_ffn_w1_w"]).T)
    g["w2t"] = np.ascontiguousarray(_f32c(inputs["L0_ffn_w2_w"]).T)
    g["h1t"] = np.ascontiguousarray(_f32c(inputs["L0_ffn_h1_w"]).T)
    g["h2t"] = np.ascontiguousarray(_f32c(inputs["L0_ffn_h2_w"]).T)
    g["w3t"] = np.ascontiguousarray(_f32c(inputs["L0_ffn_w3_w"]).T)
    g["s1"] = np.ascontiguousarray(_f32c(inputs["L0_ffn_w1_s"]).reshape(NHC, 128).T)
    g["s2"] = np.ascontiguousarray(_f32c(inputs["L0_ffn_w2_s"]).reshape(NHC, 128).T)
    g["sh1"] = np.ascontiguousarray(_f32c(inputs["L0_ffn_h1_s"]).reshape(NHC, 128).T)
    g["sh2"] = np.ascontiguousarray(_f32c(inputs["L0_ffn_h2_s"]).reshape(NHC, 128).T)
    g["s3"] = np.ascontiguousarray(_f32c(inputs["L0_ffn_w3_s"]).reshape(NCH, 128).T)

    row = np.arange(128)[:, None].astype(np.float32)
    col = np.arange(128)[None, :].astype(np.float32)
    g["mask_diag"] = np.where(np.abs(col - row) <= 64, 0.0, MASKV).astype(BF)
    g["mask_B"] = np.where(row < 64,
                           np.where(row <= col - 64, 0.0, MASKV),
                           np.where(row >= col + 64, 0.0, MASKV)).astype(BF)
    ub = np.arange(896)[None, :].astype(np.float32)
    g["mask_big"] = np.where(ub - 384 >= row, 0.0, MASKV).astype(BF)
    g["ident"] = np.eye(128, dtype=np.float32).astype(BF)
    g["ones_row"] = np.ones((1, 128), np.float32)
    g["ones_rb"] = np.ones((1, 512), np.float32).astype(BF)
    g["ones_col"] = np.ones((128, 1), np.float32)
    per_core = [{"xT": np.ascontiguousarray(x[b].T)} for b in range(x.shape[0])]
    return g, per_core


def build_program(n_sub=4, do_prep=True):
    nc = bass.Bass("TRN2", target_bir_lowering=False)
    sch = Sched()

    def din(name, shape, dtype=F32):
        return nc.dram_tensor(name, list(shape), dtype, kind="ExternalInput").ap()

    xT = din("xT", [D, S])
    IN = {}
    for nm, shp, dt in [
            ("wa_qk", [D, 2 * D], F32), ("wa_v", [D, D], F32),
            ("ba_q", [128, NCH], F32), ("ba_k", [128, NCH], F32),
            ("ba_v", [1, D], F32), ("wa_out", [D, D], F32), ("ba_out", [1, D], F32),
            ("wb_q", [D, D], F32), ("wb_k", [D, D], F32), ("wb_v", [D, D], F32),
            ("wb_o", [D, D], F32),
            ("w1t", [D, DH], F32), ("w2t", [D, DH], F32), ("h1t", [DH, DH], F32),
            ("h2t", [DH, DH], F32), ("w3t", [DH, D], F32),
            ("s1", [128, NHC], F32), ("s2", [128, NHC], F32), ("sh1", [128, NHC], F32),
            ("sh2", [128, NHC], F32), ("s3", [128, NCH], F32),
            ("mask_diag", [128, 128], BF16), ("mask_B", [128, 128], BF16),
            ("mask_big", [128, 896], BF16), ("ident", [128, 128], BF16),
            ("ones_row", [1, 128], F32), ("ones_rb", [1, 512], BF16),
            ("ones_col", [128, 1], F32)]:
        IN[nm] = din(nm, shp, dt)
    yT = nc.dram_tensor("yT", [D, S], F32, kind="ExternalOutput").ap()

    ctx = ExitStack()

    def sb(name, shape, dtype):
        return ctx.enter_context(nc.sbuf_tensor(name, shape, dtype))

    x_fm = sb("x_fm", [128, NCH * S], F32)
    xln = sb("xln", [128, NCH * CG], BF16)
    big = sb("big", [128, 16384 + 16 * 16 * 66 + 4096 + 4096], BF16)
    OFF_K, OFF_V = 0, 16384
    OFF_Q = OFF_V + 16 * 16 * 66
    OFF_STK = OFF_Q + 4096
    VSLOT = 66
    VSTR = H * VSLOT
    wbuf = sb("wbuf", [128, RING * PSLOT], BF16)
    rows_a = sb("rows_a", [1, 2 * CG], F32)
    rows_b = sb("rows_b", [1, 2 * CG], F32)
    rows1 = sb("rows1", [1, 1024], F32)
    sqb = sb("sqb", [128, CG], BF16)
    t1 = sb("t1", [128, CG], BF16)
    fscr = sb("fscr", [128, CG], F32)
    srow = sb("srow", [128, CG], F32)
    bcs = sb("bcs", [128, CG], F32)
    pbuf = sb("pbuf", [128, 4 * CG], BF16)
    m_diag = sb("m_diag", [128, 128], BF16)
    m_B = sb("m_B", [128, 128], BF16)
    m_big = sb("m_big", [128, 896], BF16)
    ident_s = sb("ident_s", [128, 128], BF16)
    onesr = sb("onesr", [1, 128], F32)
    onesrb = sb("onesrb", [1, 512], BF16)
    onesc = sb("onesc", [128, 1], F32)
    onescb = sb("onescb", [128, 1], BF16)
    xbt = sb("xbt", [128, CG], BF16)
    ones64 = sb("ones64", [128, 64], F32)
    bq_s = sb("bq_s", [128, NCH], F32)
    bk_s = sb("bk_s", [128, NCH], F32)
    bks2 = sb("bks2", [128, NCH], F32)
    bav_s = sb("bav_s", [1, D], BF16)
    bao_s = sb("bao_s", [1, D], BF16)
    s1_s = sb("s1_s", [128, NHC], F32)
    s2_s = sb("s2_s", [128, NHC], F32)
    sh1_s = sb("sh1_s", [128, NHC], F32)
    sh2_s = sb("sh2_s", [128, NHC], F32)
    s3_s = sb("s3_s", [128, NCH], F32)
    c12 = sb("c12", [128, 1], F32)
    ceps = sb("ceps", [128, 1], F32)
    assert nc.sbuf_bytes_remaining >= 0, f"SBUF over: {nc.sbuf_bytes_remaining}"

    PSB = [ctx.enter_context(nc.psum_tensor(f"psb{i}", [128, 512], F32))
           for i in range(8)]
    bank_reader = [None] * 8
    bank_writer = [None] * 8

    sdX, sdM, sdO, sdB = Sem("dX"), Sem("dM"), Sem("dO"), Sem("dB")
    sdW = [Sem(f"dW{i}") for i in range(RING)]
    sdWc = [Sem(f"dWc{i}") for i in range(RING)]
    sdK = [Sem(f"dK{i}") for i in range(RING)]
    sT, sV, sC = Sem("T", "tensor"), Sem("V", "vector"), Sem("C", "scalar")
    ALLSEMS = [sdX, sdM, sdO, sdB] + sdW + sdWc + sdK + [sT, sV, sC]

    def dma(eng, s, dst, src, waits=(), tag=""):
        sch.ev(eng, lambda e, d=dst, r=src: e.dma_start(d, r),
               waits=waits, incs=[(s, 16)], tag=tag)
        return s.mark()

    def vec(fn, waits=(), tag=""):
        sch.ev("vector", fn, waits=waits, incs=[(sV, 1)], tag=tag)
        return sV.mark()

    def sca(fn, waits=(), tag=""):
        sch.ev("scalar", fn, waits=waits, incs=[(sC, 1)], tag=tag)
        return sC.mark()

    class Accum:

        def __init__(self, bank, tag=""):
            self.bank, self.first, self.tag = bank, True, tag
            self.started = set()

        def add(self, o, l, r, waits=(), last=False, inc=False, region="d",
                stop=None):
            w = list(waits)
            if self.first and bank_reader[self.bank] is not None:
                w.append(bank_reader[self.bank])
            self.first = False
            st = region not in self.started
            self.started.add(region)
            sp = last if stop is None else stop
            sch.ev("tensor",
                   lambda e, o=o, l=l, r=r, st=st, sp=sp:
                       e.matmul(o, l, r, start=st, stop=sp),
                   waits=w, incs=[(sT, 1)] if (last or inc) else [], tag=self.tag)
            if last:
                bank_writer[self.bank] = sT.mark()
                return bank_writer[self.bank]
            return sT.mark() if inc else None

    def mm_group(bank, mms, tag=""):
        acc = Accum(bank, tag)
        mk = None
        for i, (o, l, r, w) in enumerate(mms):
            mk = acc.add(o, l, r, waits=w, last=(i == len(mms) - 1))
        return mk

    def evac_done(bank, mark):
        bank_reader[bank] = mark

    class Plan:

        def __init__(self, name, src, din_, dout_, kind):
            self.name, self.kind = name, kind
            self.kch = din_ // 128
            csz = 128 if kind == "lhsT" else 512
            self.csz = csz
            self.nmaj = dout_ // csz
            self.epm = self.kch * csz
            total = self.nmaj * self.epm
            assert total % PSLOT == 0
            self.nblk = total // PSLOT
            if self.epm <= PSLOT:
                assert PSLOT % self.epm == 0
                self.mper = PSLOT // self.epm
                self.kpb = self.kch
            else:
                assert self.epm % PSLOT == 0
                self.mper = 0
                self.kpb = PSLOT // csz
            self.t = nc.dram_tensor("pk_" + name, [self.nblk, 128, PSLOT], BF16,
                                    kind="Internal").ap()
            self.blk_mark = [None] * self.nblk
            self.view = src.rearrange("(k p) m -> p k m", p=128)

        def block_src(self, b):
            if self.mper:
                m0 = b * self.mper
                return self.view[:, :, m0 * self.csz:(m0 + self.mper) * self.csz]
            nbpm = self.kch // self.kpb
            maj, kb = b // nbpm, b % nbpm
            return self.view[:, kb * self.kpb:(kb + 1) * self.kpb,
                             maj * self.csz:(maj + 1) * self.csz]

        def tile_loc(self, maj, k):
            if self.mper:
                blk = maj // self.mper
                off = k * self.mper * self.csz + (maj % self.mper) * self.csz
                return blk, off
            nbpm = self.kch // self.kpb
            return maj * nbpm + k // self.kpb, (k % self.kpb) * self.csz

    PAQK = Plan("aqk", IN["wa_qk"], D, 2 * D, "lhsT")
    PAV = Plan("av", IN["wa_v"], D, D, "rhs")
    PAOUT = Plan("aout", IN["wa_out"], D, D, "lhsT")
    PW1 = Plan("w1", IN["w1t"], D, DH, "lhsT")
    PW2 = Plan("w2", IN["w2t"], D, DH, "lhsT")
    PH1 = Plan("h1", IN["h1t"], DH, DH, "lhsT")
    PH2 = Plan("h2", IN["h2t"], DH, DH, "lhsT")
    PW3 = Plan("w3", IN["w3t"], DH, D, "lhsT")
    PBK = Plan("bk", IN["wb_k"], D, D, "lhsT")
    PBV = Plan("bv", IN["wb_v"], D, D, "rhs")
    PBQ = Plan("bq", IN["wb_q"], D, D, "lhsT")
    PBO = Plan("bo", IN["wb_o"], D, D, "lhsT")

    for ch in range(NCH):
        dma("sync", sdX, x_fm[:, ch * S:(ch + 1) * S],
            xT[ch * 128:(ch + 1) * 128, :], tag=f"ldx{ch}")
    mk_x = sdX.mark()
    for dst, key in [(m_diag, "mask_diag"), (m_B, "mask_B"), (m_big, "mask_big"),
                     (ident_s, "ident"), (onesr, "ones_row"), (onesrb, "ones_rb"),
                     (onesc, "ones_col"), (bq_s, "ba_q"), (bk_s, "ba_k"),
                     (s1_s, "s1"), (s2_s, "s2"),
                     (sh1_s, "sh1"), (sh2_s, "sh2"), (s3_s, "s3")]:
        dma("sync", sdM, dst[:], IN[key], tag="ldmisc")
    mk_misc = sdM.mark()
    vec(lambda e: e.memset(c12[:], -SHIFT), tag="c12")
    vec(lambda e: e.memset(ceps[:], EPS), tag="ceps")
    vec(lambda e: e.memset(ones64[:], 1.0), tag="ones64")
    vec(lambda e: e.tensor_scalar(bks2[:], bk_s[:], 0.125, None, ALU.mult),
        waits=[mk_misc], tag="bks2")
    mk_consts = vec(lambda e: e.tensor_copy(onescb[:], onesc[:]),
                    waits=[mk_misc], tag="onescb")
    mk_bav = dma("gpsimd", sdB, bav_s[:], IN["ba_v"], tag="bavc")
    mk_bao = dma("gpsimd", sdB, bao_s[:], IN["ba_out"], tag="baoc")


    stream = {"i": 0, "freed": []}

    class WCursor:

        def __init__(self, plan, tag):
            self.plan, self.tag = plan, tag
            self.blk, self.sbase, self.lmk = None, None, None
            self.load_idx = None
            self.pending = []
            self.last_mark = None
            self.pack_pending = None

        def _depart(self):
            idx = len(stream["freed"])
            assert idx == self.load_idx, (idx, self.load_idx, self.tag)
            stream["freed"].append(None)
            self.pending.append((idx, self.pack_pending))

        def tile(self, maj, k):
            b, off = self.plan.tile_loc(maj, k)
            if b != self.blk:
                if self.blk is not None:
                    self._depart()
                i = stream["i"]
                stream["i"] += 1
                slot = i % RING
                w = []
                if i >= RING:
                    fm = stream["freed"][i - RING]
                    assert fm is not None, f"ring free unresolved at load {i}"
                    w += fm
                slot_ap = wbuf[:, slot * PSLOT:(slot + 1) * PSLOT]
                if self.plan.blk_mark[b] is None:
                    self.lmk = dma("gpsimd", sdWc[slot], slot_ap,
                                   self.plan.block_src(b), waits=w,
                                   tag=f"wc:{self.plan.name}:{b}")
                    pk_mk = dma("sync", sdK[slot], self.plan.t[b], slot_ap,
                                waits=[self.lmk], tag=f"wp:{self.plan.name}:{b}")
                    self.plan.blk_mark[b] = pk_mk
                    self.pack_pending = pk_mk
                else:
                    eng = "sync" if slot % 2 == 0 else "gpsimd"
                    self.lmk = dma(eng, sdW[slot], slot_ap, self.plan.t[b],
                                   waits=w + [self.plan.blk_mark[b]],
                                   tag=f"wl:{self.plan.name}:{b}")
                    self.pack_pending = None
                self.blk, self.sbase, self.load_idx = b, slot * PSLOT, i
            return (wbuf[:, self.sbase + off:self.sbase + off + self.plan.csz],
                    self.lmk)

        def used(self):
            self.last_mark = sT.mark()
            for idx, pk in self.pending:
                stream["freed"][idx] = [self.last_mark] + ([pk] if pk else [])
            self.pending = []

        def close(self):
            if self.blk is not None:
                self.used()
                idx = len(stream["freed"])
                assert idx == self.load_idx
                stream["freed"].append([self.last_mark] +
                                       ([self.pack_pending] if self.pack_pending
                                        else []))
                self.blk = None

    hz = {"xln_rd": None, "sq_rd": [None, None], "rows_rd": None,
          "stk_rd": None, "q_rd": None, "pb_rd": [None] * 4,
          "fscr_C": None, "fscr_V": None,
          "srow_rd": [None, None], "bcs_rd": [None, None]}
    res = {}
    vev = {}

    def ln_stats(c, sub, tag):
        rw = [res.get((sub - 1, c))] if (sub - 1, c) in res else []
        accA = Accum(6, tag + ":sx")
        accB = Accum(7, tag + ":sxx")
        xslots = [xbt[:], fscr[:].bitcast(BF16)[:, 0:CG]]
        sslots = [sqb[:], t1[:]]
        mkA_s = [hz["sq_rd"][0], hz["fscr_V"]]
        mkB_s = [hz["sq_rd"][1], hz["sq_rd"][1]]
        mkA = mkB = None
        for ch in range(NCH):
            sl = ch % 2
            xsl = x_fm[:, ch * S + c * CG: ch * S + (c + 1) * CG]
            w = (list(rw) + [mk_x]) if ch == 0 else []
            if mkA_s[sl] is not None:
                w.append(mkA_s[sl])
            mk_c = vec(lambda e, o=xslots[sl], a=xsl: e.tensor_copy(o, a),
                       waits=w, tag=tag + f":xc{ch}")
            w2 = [mk_c]
            if mkB_s[sl] is not None:
                w2.append(mkB_s[sl])
            smk = vec(lambda e, o=sslots[sl], a=xslots[sl]:
                      e.tensor_tensor(o, a, a, ALU.mult),
                      waits=w2, tag=tag + f":sq{ch}")
            wA = [mk_c] + ([mk_consts] if ch == 0 else [])
            mkA = accA.add(PSB[6][0:1, :], onescb[:, 0:1], xslots[sl], waits=wA,
                           last=(ch == NCH - 1), inc=True)
            mkA_s[sl] = mkA
            mkB = accB.add(PSB[7][0:1, :], onescb[:, 0:1], sslots[sl], waits=[smk],
                           last=(ch == NCH - 1), inc=True)
            mkB_s[sl] = mkB
        hz["sq_rd"][0] = mkA
        hz["sq_rd"][1] = mkB
        hz["fscr_V"] = mkA
        rm, rv = rows1[0:1, 0:CG], rows1[0:1, CG:2 * CG]
        msq = srow[0:1, 0:CG]
        mk = vec(lambda e: e.tensor_scalar(rm, PSB[6][0:1, :], INV_D, None, ALU.mult),
                 waits=[mkA], tag=tag + ":m")
        mk = vec(lambda e: e.tensor_tensor(msq, rm, rm, ALU.mult), waits=[mk],
                 tag=tag + ":msq")
        mk = vec(lambda e: e.scalar_tensor_tensor(rv, PSB[7][0:1, :], INV_D, msq,
                                                  ALU.mult, ALU.subtract),
                 waits=[mk, mkB], tag=tag + ":var")
        evac_done(6, mk)
        evac_done(7, mk)
        mk = sca(lambda e: e.activation(msq, rv, AF.Ln, bias=ceps[0:1, 0:1]),
                 waits=[mk, mk_consts], tag=tag + ":ln")
        aslc = rows_a[0:1, (c % 2) * CG:(c % 2 + 1) * CG]
        bslc = rows_b[0:1, (c % 2) * CG:(c % 2 + 1) * CG]
        w = [mk] + ([hz["rows_rd"]] if hz["rows_rd"] else [])
        mka = sca(lambda e: e.activation(aslc, msq, AF.Exp, scale=-0.5),
                  waits=w, tag=tag + ":a")
        mkb = vec(lambda e: e.scalar_tensor_tensor(bslc, rm, -1.0, aslc,
                                                   ALU.mult, ALU.mult),
                  waits=[mka], tag=tag + ":b")
        return [mka, mkb]

    def ln_apply(c, row_marks, tag):
        aslc = rows_a[0:1, (c % 2) * CG:(c % 2 + 1) * CG]
        bslc = rows_b[0:1, (c % 2) * CG:(c % 2 + 1) * CG]
        mkA = mm_group(6, [(PSB[6][:, :], onesr[0:1, 0:128], aslc,
                            list(row_marks) + [mk_misc])], tag + ":bca")
        mkB = mm_group(7, [(PSB[7][:, :], onesr[0:1, 0:128], bslc,
                            list(row_marks) + [mk_misc])], tag + ":bcb")
        hz["rows_rd"] = sT.mark()
        last = None
        for ch in range(NCH):
            xsl = x_fm[:, ch * S + c * CG: ch * S + (c + 1) * CG]
            xo = xln[:, ch * CG:(ch + 1) * CG]
            w = [mkA]
            if last is not None:
                w.append(last)
            if ch == 0 and hz["xln_rd"]:
                w.append(hz["xln_rd"])
            if ch == 0 and hz["fscr_C"]:
                w.append(hz["fscr_C"])
            mk1 = vec(lambda e, a=xsl: e.tensor_tensor(fscr[:], a, PSB[6][:, :], ALU.mult),
                      waits=w, tag=tag + f":ap{ch}")
            last = vec(lambda e, o=xo: e.tensor_tensor(o, fscr[:], PSB[7][:, :], ALU.add),
                       waits=[mk1, mkB], tag=tag + f":ab{ch}")
        hz["fscr_V"] = last
        evac_done(6, last)
        evac_done(7, last)
        return [last]

    def proj_xln(plan, mlist, xm, evac, tag, banks=(0, 1)):
        cur = WCursor(plan, tag)
        for mi, m in enumerate(mlist):
            bank = banks[mi % 2]
            mms = []
            for k in range(plan.kch):
                lh, lmk = cur.tile(m, k)
                w = [lmk]
                if k == 0:
                    w += list(xm)
                mms.append((PSB[bank][:, :], lh, xln[:, k * CG:(k + 1) * CG], w))
            mkw = mm_group(bank, mms, tag + f":m{m}")
            cur.used()
            mk = evac(mi, m, bank, mkw)
            evac_done(bank, mk)
        cur.close()
        hz["xln_rd"] = sT.mark()

    def v_gen(c, plan, with_bias, xm, tag):
        cur = WCursor(plan, tag)
        for t4 in range(4):
            tt_ = c * 4 + t4
            for n in range(2):
                bank = 2 + (t4 * 2 + n) % 2
                mms = []
                for k in range(NCH):
                    rh, lmk = cur.tile(n, k)
                    lh = xln[:, k * CG + t4 * 128: k * CG + t4 * 128 + 128]
                    w = [lmk]
                    if k == 0:
                        w += list(xm)
                    mms.append((PSB[bank][:, :], lh, rh, w))
                if with_bias:
                    mms.append((PSB[bank][:, :], onesrb[0:1, 0:128],
                                bav_s[0:1, n * 512:(n + 1) * 512], [mk_bao]))
                mkw = mm_group(bank, mms, tag + f":V{t4}.{n}")
                cur.used()
                lastmk = None
                for hh in range(8):
                    h = n * 8 + hh
                    dst = big[:, OFF_V + tt_ * VSTR + h * VSLOT:
                              OFF_V + tt_ * VSTR + h * VSLOT + 64]
                    w = [mkw] if hh == 0 else []
                    lastmk = vec(lambda e, d=dst,
                                 s=PSB[bank][:, hh * 64:(hh + 1) * 64]:
                                 e.tensor_copy(d, s), waits=w,
                                 tag=tag + f":Vev{t4}.{n}.{hh}")
                vev[(tt_, n)] = lastmk
                evac_done(bank, lastmk)
        cur.close()
        hz["xln_rd"] = sT.mark()

    def attn_proj(plan, sub, c, bias, stk_mark, tag):
        cur = WCursor(plan, tag)
        last_res = None
        for m in range(NCH):
            bank = m % 2
            mms = []
            for kp in range(NCH):
                lh, lmk = cur.tile(m, kp)
                stk = big[:, OFF_STK + kp * CG: OFF_STK + (kp + 1) * CG]
                w = [lmk]
                if kp == 0:
                    w.append(stk_mark)
                mms.append((PSB[bank][:, :], lh, stk, w))
            if bias:
                mms.append((PSB[bank][:, :], bao_s[0:1, m * 128:(m + 1) * 128],
                            onesrb[0:1, 0:512], [mk_bao]))
            mkw = mm_group(bank, mms, tag + f":m{m}")
            cur.used()
            xsl = x_fm[:, m * S + c * CG: m * S + (c + 1) * CG]
            last_res = vec(lambda e, o=xsl, ps=PSB[bank][:, :]:
                           e.tensor_tensor(o, o, ps, ALU.add),
                           waits=[mkw], tag=tag + f":res{m}")
            evac_done(bank, last_res)
        cur.close()
        hz["stk_rd"] = sT.mark()
        res[(sub, c)] = last_res

    def norm_stack(h, t4, width, b_o, pv_mk, tag):
        par = h % 2
        rrow = 64 if par == 0 else 32
        b_bc = 6 + par
        w = [pv_mk]
        if hz["srow_rd"][par]:
            w.append(hz["srow_rd"][par])
        mk_r = vec(lambda e: e.reciprocal(srow[rrow:rrow + 1, 0:width],
                                          PSB[b_o][rrow:rrow + 1, 0:width]),
                   waits=w, tag=tag + ":rcp")
        mk_bc = mm_group(b_bc, [((PSB[b_bc][0:64, 0:width] if par == 0
                                  else PSB[b_bc][64:128, 0:width]),
                                 ones64[rrow:rrow + 1, 0:64],
                                 srow[rrow:rrow + 1, 0:width], [mk_r, mk_consts])],
                        tag + ":bc")
        hz["srow_rd"][par] = sT.mark()
        rb = 0 if par == 0 else 64
        w = [mk_bc]
        if hz["bcs_rd"][par]:
            w.append(hz["bcs_rd"][par])
        mk_cp = sca(lambda e: e.copy(bcs[rb:rb + 64, 0:width],
                                     PSB[b_bc][rb:rb + 64, 0:width]),
                    waits=w, tag=tag + ":cpy")
        evac_done(b_bc, mk_cp)
        hp = h // 2
        if width == 128:
            dst = big[:, OFF_STK + hp * CG + t4 * 128:
                      OFF_STK + hp * CG + t4 * 128 + 128]
        else:
            dst = big[:, OFF_STK + hp * CG: OFF_STK + (hp + 1) * CG]
        dst = dst[rb:rb + 64, :]
        vrows = PSB[b_o][0:64, 0:width] if par == 0 else PSB[b_o][64:128, 0:width]
        w = [mk_cp]
        if hz["stk_rd"]:
            w.append(hz["stk_rd"])
        mk_n = vec(lambda e: e.tensor_tensor(dst, vrows, bcs[rb:rb + 64, 0:width],
                                             ALU.mult),
                   waits=w, tag=tag + ":mul")
        hz["bcs_rd"][par] = mk_n
        evac_done(b_o, mk_n)
        return mk_n

    def attn_a(sub, arena_entry):
        mk_vones = vec(lambda e: e.memset(big[:, OFF_V:OFF_V + NT * VSTR], 1.0),
                       waits=[mk_consts] + list(arena_entry), tag="A:vones")
        mk_pz = mk_vones
        for sl in range(2):
            mk_pz = vec(lambda e, a=pbuf[0:64, sl * CG + 128: sl * CG + 256]:
                        e.memset(a, 0.0), tag="A:pz")
            mk_pz = vec(lambda e, a=pbuf[64:128, sl * CG + 256: sl * CG + 384]:
                        e.memset(a, 0.0), tag="A:pz")
        qmk = {}
        p0v = {}

        rowmk = {}

        def part0(c):
            rmk = ln_stats(c, sub, f"A:ln{c}")
            rowmk[c] = rmk
            xm = ln_apply(c, rmk, f"A:lap{c}")
            if A_STAGE == "ln":
                return

            def kev(mi, m, bank, mkw):
                pr = m - 8
                dst = big[:, OFF_K + pr * S + c * CG: OFF_K + pr * S + (c + 1) * CG]
                w = [mkw, mk_consts] + list(arena_entry)
                return vec(lambda e, d=dst, b=bank: e.tensor_scalar(
                    d, PSB[b][:, :], 0.125, bks2[:, pr:pr + 1], ALU.mult, ALU.add),
                    waits=w, tag=f"A:Kev{c}.{pr}")
            proj_xln(PAQK, list(range(8, 16)), xm, kev, f"A:K{c}")
            v_gen(c, PAV, True, list(xm) + [mk_vones], f"A:V{c}")
            return

        def part1(c):
            if A_STAGE in ("ln", "kv"):
                return
            xm = ln_apply(c, rowmk[c], f"A:lapq{c}")

            def qev(mi, m, bank, mkw):
                dst = big[:, OFF_Q + m * CG: OFF_Q + (m + 1) * CG]
                w = [mkw, mk_misc]
                if hz["q_rd"]:
                    w.append(hz["q_rd"])
                mk = vec(lambda e, d=dst, b=bank, m=m: e.tensor_scalar(
                    d, PSB[b][:, :], bq_s[:, m:m + 1], None, ALU.add),
                    waits=w, tag=f"A:Qev{c}.{m}")
                qmk[m] = mk
                return mk
            proj_xln(PAQK, list(range(8)), xm, qev, f"A:Q{c}")
            if A_STAGE == "q":
                return
            qlast = qmk[7]
            stk_mark = None
            items = [(c * 4 + t4, t4, h) for t4 in range(4) for h in range(H)]
            s1 = s2 = None
            for it in items + [None, None]:
                st = (a_scores(c, it[0], it[1], it[2], qlast,
                               f"A:t{it[0]}h{it[2]}") if it is not None else None)
                pvst = a_pv(s1) if s1 is not None else None
                if s2 is not None:
                    stk_mark = norm_stack(s2["h"], s2["t4"], 128, s2["b_o"],
                                          s2["pv_mk"], s2["tag"])
                s1, s2 = st, pvst
            hz["q_rd"] = sT.mark()
            if A_STAGE == "scores":
                return
            attn_proj(PAOUT, sub, c, True, stk_mark, f"A:pj{c}")

        def a_scores(c, t, t4, h, qlast, tag):
            hp, hr = h // 2, 64 * (h % 2)
            idx = t * H + h
            b_d, b_B = idx % 2, 2 + idx % 2
            kbase = OFF_K + hp * S
            qsl = big[:, OFF_Q + hp * CG + t4 * 128: OFF_Q + hp * CG + t4 * 128 + 128]
            qsl = qsl[hr:hr + 64, :]
            lh = big[:, kbase + t * 128: kbase + t * 128 + 128][hr:hr + 64, :]
            kcov = p0v[min(c + 1, NCG - 1)]
            mkd = mm_group(b_d, [
                (PSB[b_d][:, 0:128], lh, qsl, [qlast, kcov]),
                (PSB[b_d][:, 0:128], ident_s[:, :], m_diag[:, :], [mk_misc])],
                tag + ":d")
            accB = Accum(b_B, tag + ":B")
            nparts = int(t < NT - 1) + int(t > 0)
            np_i = 0
            if t < NT - 1:
                lh2 = big[:, kbase + (t + 1) * 128: kbase + (t + 1) * 128 + 64]
                accB.add(PSB[b_B][0:64, 0:128], lh2[hr:hr + 64, :], qsl, region="n")
                np_i += 1
                accB.add(PSB[b_B][0:64, 0:128], ident_s[:, 0:64], m_B[:, :],
                         waits=[mk_misc], region="n", last=(np_i == nparts),
                         stop=True)
            if t > 0:
                lh3 = big[:, kbase + t * 128 - 64: kbase + t * 128]
                accB.add(PSB[b_B][64:128, 0:128], lh3[hr:hr + 64, :], qsl, region="p")
                np_i += 1
                accB.add(PSB[b_B][64:128, 0:128], ident_s[:, 64:128], m_B[:, :],
                         waits=[mk_misc], region="p", last=(np_i == nparts),
                         stop=True)
            mkB = bank_writer[b_B]
            ps = idx % 2
            p_d = pbuf[:, ps * CG: ps * CG + 128]
            p_prev = pbuf[:, ps * CG + 128: ps * CG + 256]
            p_next = pbuf[:, ps * CG + 256: ps * CG + 384]
            w = [mkd, mk_pz] + ([hz["pb_rd"][ps]] if hz["pb_rd"][ps] else [])
            mk_pd = sca(lambda e, o=p_d: e.activation(o, PSB[b_d][:, 0:128], AF.Exp,
                                                      bias=c12[:, 0:1]),
                        waits=w + [mk_consts], tag=tag + ":ed")
            evac_done(b_d, mk_pd)
            mk_pB = mk_pd
            if t > 0:
                mk_pB = sca(lambda e, o=p_prev[64:128, :], a=PSB[b_B][64:128, 0:128],
                            bb=c12[64:128, 0:1]:
                            e.activation(o, a, AF.Exp, bias=bb),
                            waits=[mkB], tag=tag + ":eBp")
            if t < NT - 1:
                mk_pB = sca(lambda e, o=p_next[0:64, :], a=PSB[b_B][0:64, 0:128],
                            bb=c12[0:64, 0:1]:
                            e.activation(o, a, AF.Exp, bias=bb),
                            waits=[mkB], tag=tag + ":eBn")
            evac_done(b_B, mk_pB)
            return dict(t=t, t4=t4, h=h, ps=ps, p_d=p_d, p_prev=p_prev,
                        p_next=p_next, mk_pd=mk_pd, mk_pB=mk_pB, tag=tag)

        def a_pv(st):
            t, t4, h, ps = st["t"], st["t4"], st["h"], st["ps"]
            b_o = 4 + (t * H + h) % 2

            def vs(j, a, bb):
                return big[:, OFF_V + j * VSTR + h * VSLOT + a:
                           OFF_V + j * VSTR + h * VSLOT + bb]
            vw = [vev[(t, h // 8)], st["mk_pd"], st["mk_pB"]]
            if t > 0:
                vw.append(vev[(t - 1, h // 8)])
            if t < NT - 1:
                vw.append(vev[(t + 1, h // 8)])
            acc = Accum(b_o, st["tag"] + ":pv")
            p_d, p_prev, p_next = st["p_d"], st["p_prev"], st["p_next"]
            mms = []
            if h % 2 == 0:
                mms.append((PSB[b_o][0:65, 0:128], vs(t, 0, 65), p_d, vw, "v"))
                if t > 0:
                    mms.append((PSB[b_o][0:65, 0:128], vs(t - 1, 0, 65), p_prev,
                                [], "v"))
                if t < NT - 1:
                    mms.append((PSB[b_o][0:65, 0:128], vs(t + 1, 0, 65), p_next,
                                [], "v"))
            else:
                mms.append((PSB[b_o][64:128, 0:128], vs(t, 0, 64), p_d, vw, "v"))
                mms.append((PSB[b_o][32:33, 0:128], vs(t, 64, 65), p_d, [], "s"))
                if t > 0:
                    mms.append((PSB[b_o][64:128, 0:128], vs(t - 1, 0, 64), p_prev,
                                [], "v"))
                    mms.append((PSB[b_o][32:33, 0:128], vs(t - 1, 64, 65), p_prev,
                                [], "s"))
                if t < NT - 1:
                    mms.append((PSB[b_o][64:128, 0:128], vs(t + 1, 0, 64), p_next,
                                [], "v"))
                    mms.append((PSB[b_o][32:33, 0:128], vs(t + 1, 64, 65), p_next,
                                [], "s"))
            last_of_region = {}
            for i, (_, _, _, _, rg) in enumerate(mms):
                last_of_region[rg] = i
            pv_mk = None
            for i, (o, l, r, w, rg) in enumerate(mms):
                pv_mk = acc.add(o, l, r, waits=w, region=rg,
                                last=(i == len(mms) - 1),
                                stop=(last_of_region[rg] == i))
            hz["pb_rd"][ps] = sT.mark()
            return dict(h=h, t4=t4, b_o=b_o, pv_mk=pv_mk, tag=st["tag"])

        order = [("p0", 0), ("p0", 1), ("p1", 0), ("p0", 2), ("p1", 1),
                 ("p0", 3), ("p1", 2), ("p1", 3)]
        for kind, c in order:
            if kind == "p0":
                part0(c)
                p0v[c] = sV.mark()
            else:
                part1(c)
        return sT.mark()

    def ffn(sub, arena_entry):
        gmk = [None] * NHC
        hmk = [None] * NHC
        g2mk = [None] * NHC
        for c in range(NCG):
            rmk = ln_stats(c, sub, f"F{sub}:ln{c}")
            xm = ln_apply(c, rmk, f"F{sub}:lap{c}")
            cur1 = WCursor(PW1, f"F{sub}:w1.{c}")
            cur2 = WCursor(PW2, f"F{sub}:w2.{c}")
            for m in range(NHC):
                bA, bB = m % 2, 2 + m % 2
                mmsA = []
                mmsB = []
                for k in range(NCH):
                    lh1, l1 = cur1.tile(m, k)
                    w = [l1]
                    if k == 0:
                        w += list(xm)
                    mmsA.append((PSB[bA][:, :], lh1, xln[:, k * CG:(k + 1) * CG], w))
                mkA = mm_group(bA, mmsA, f"F{sub}:w1.{c}.{m}")
                cur1.used()
                for k in range(NCH):
                    lh2, l2 = cur2.tile(m, k)
                    mmsB.append((PSB[bB][:, :], lh2, xln[:, k * CG:(k + 1) * CG],
                                 [l2]))
                mkB = mm_group(bB, mmsB, f"F{sub}:w2.{c}.{m}")
                cur2.used()
                w = [mkA, mk_misc] + ([hz["fscr_V"]] if hz["fscr_V"] else [])
                mk_s = sca(lambda e, b=bA, m=m: e.activation(
                    fscr[:], PSB[b][:, :], AF.Silu, scale=s1_s[:, m:m + 1]),
                    waits=w, tag=f"F{sub}:silu{c}.{m}")
                hz["fscr_C"] = mk_s
                evac_done(bA, mk_s)
                dst = big[:, m * CG:(m + 1) * CG]
                w = [mk_s, mkB]
                if m == 0 and c == 0:
                    w += list(arena_entry)
                mk_g = vec(lambda e, d=dst, b=bB, m=m: e.scalar_tensor_tensor(
                    d, fscr[:], s2_s[:, m:m + 1], PSB[b][:, :], ALU.mult, ALU.mult),
                    waits=w, tag=f"F{sub}:g{c}.{m}")
                hz["fscr_V"] = mk_g
                evac_done(bB, mk_g)
                gmk[m] = mk_g
            cur1.close()
            cur2.close()
            hz["xln_rd"] = sT.mark()
            cur = WCursor(PH1, f"F{sub}:h1.{c}")
            for m in range(NHC):
                bank = m % 2
                mms = []
                for k in range(NHC):
                    lh, lmk = cur.tile(m, k)
                    w = [lmk]
                    if k == 0 and m == 0:
                        w.append(gmk[NHC - 1])
                        w += list(arena_entry)
                    mms.append((PSB[bank][:, :], lh, big[:, k * CG:(k + 1) * CG], w))
                mkw = mm_group(bank, mms, f"F{sub}:h1.{c}.{m}")
                cur.used()
                dst = big[:, OFF_V + m * CG: OFF_V + (m + 1) * CG]
                hmk[m] = sca(lambda e, d=dst, b=bank, m=m: e.activation(
                    d, PSB[b][:, :], AF.Gelu, scale=sh1_s[:, m:m + 1]),
                    waits=[mkw, mk_misc], tag=f"F{sub}:gel1.{c}.{m}")
                evac_done(bank, hmk[m])
            cur.close()
            mk_h1_T = sT.mark()
            cur = WCursor(PH2, f"F{sub}:h2.{c}")
            for m in range(NHC):
                bank = m % 2
                mms = []
                for k in range(NHC):
                    lh, lmk = cur.tile(m, k)
                    w = [lmk]
                    if k == 0 and m == 0:
                        w.append(hmk[NHC - 1])
                    mms.append((PSB[bank][:, :], lh,
                                big[:, OFF_V + k * CG: OFF_V + (k + 1) * CG], w))
                mkw = mm_group(bank, mms, f"F{sub}:h2.{c}.{m}")
                cur.used()
                dst = big[:, m * CG:(m + 1) * CG]
                w = [mkw, mk_misc]
                if m == 0:
                    w.append(mk_h1_T)
                g2mk[m] = sca(lambda e, d=dst, b=bank, m=m: e.activation(
                    d, PSB[b][:, :], AF.Gelu, scale=sh2_s[:, m:m + 1]),
                    waits=w, tag=f"F{sub}:gel2.{c}.{m}")
                evac_done(bank, g2mk[m])
            cur.close()
            cur = WCursor(PW3, f"F{sub}:w3.{c}")
            last_res = None
            for m in range(NCH):
                bank = m % 2
                mms = []
                for k in range(NHC):
                    lh, lmk = cur.tile(m, k)
                    w = [lmk]
                    if k == 0 and m == 0:
                        w.append(g2mk[NHC - 1])
                    mms.append((PSB[bank][:, :], lh, big[:, k * CG:(k + 1) * CG], w))
                mkw = mm_group(bank, mms, f"F{sub}:w3.{c}.{m}")
                cur.used()
                xsl = x_fm[:, m * S + c * CG: m * S + (c + 1) * CG]
                last_res = vec(lambda e, o=xsl, b=bank, m=m: e.scalar_tensor_tensor(
                    o, PSB[b][:, :], s3_s[:, m:m + 1], o, ALU.mult, ALU.add),
                    waits=[mkw, mk_misc], tag=f"F{sub}:res{c}.{m}")
                evac_done(bank, last_res)
            cur.close()
            res[(sub, c)] = last_res
        return sT.mark()

    def attn_b(sub, arena_entry):
        mk_vones = vec(lambda e: e.memset(big[:, OFF_V:OFF_V + NT * VSTR], 1.0),
                       waits=[mk_consts] + list(arena_entry), tag="B:vones")
        for c in range(NCG):
            rmk = ln_stats(c, sub, f"B:ln{c}")
            xm = ln_apply(c, rmk, f"B:lap{c}")

            def kev(mi, m, bank, mkw):
                dst = big[:, OFF_K + m * S + c * CG: OFF_K + m * S + (c + 1) * CG]
                w = [mkw] + list(arena_entry)
                return sca(lambda e, d=dst, b=bank: e.activation(
                    d, PSB[b][:, :], AF.Copy, scale=0.125), waits=w,
                    tag=f"B:Kev{c}.{m}")
            proj_xln(PBK, list(range(NCH)), xm, kev, f"B:K{c}")
            v_gen(c, PBV, False, list(xm) + [mk_vones], f"B:V{c}")
            xm = ln_apply(c, rmk, f"B:lapq{c}")

            def qev(mi, m, bank, mkw):
                dst = big[:, OFF_Q + m * CG: OFF_Q + (m + 1) * CG]
                w = [mkw]
                if hz["q_rd"]:
                    w.append(hz["q_rd"])
                return sca(lambda e, d=dst, b=bank: e.activation(
                    d, PSB[b][:, :], AF.Copy), waits=w, tag=f"B:Qev{c}.{m}")
            proj_xln(PBQ, list(range(NCH)), xm, qev, f"B:Q{c}")
            qlast = sC.mark()
            stk_mark = None
            for h0 in range(0, H, 2):
                nrm = b_head_pair(c, h0, qlast, f"B:c{c}h{h0}")
                for (hh, bo, pv, tg) in nrm:
                    stk_mark = norm_stack(hh, None, 512, bo, pv, tg)
            hz["q_rd"] = sT.mark()
            attn_proj(PBO, sub, c, False, stk_mark, f"B:pj{c}")
        return sT.mark()

    def b_head_pair(c, h0, qlast, tag):
        nj = 4 * c + 4
        hs = (h0, h0 + 1)
        hp = h0 // 2
        qs, accs, pvmk = [], [], [None, None]
        for hi, h in enumerate(hs):
            hr = 64 * (h % 2)
            qs.append(big[:, OFF_Q + hp * CG: OFF_Q + (hp + 1) * CG][hr:hr + 64, :])
            accs.append(Accum(4 + hi, tag + f":pv{hi}"))

        def scores(j):
            out = []
            for hi, h in enumerate(hs):
                hr = 64 * (h % 2)
                b_s = 2 * hi + j % 2
                lh = big[:, OFF_K + hp * S + j * 128: OFF_K + hp * S + j * 128 + 128]
                mms = [(PSB[b_s][:, :], lh[hr:hr + 64, :], qs[hi], [qlast])]
                if j >= 4 * c:
                    r = j - 4 * c
                    mms.append((PSB[b_s][:, :], ident_s[:, :],
                                m_big[:, (3 - r) * 128:(3 - r) * 128 + 512],
                                [mk_misc]))
                mk_s = mm_group(b_s, mms, tag + f":s{j}.{hi}")
                ps = b_s
                p = pbuf[:, ps * CG:(ps + 1) * CG]
                w = [mk_s] + ([hz["pb_rd"][ps]] if hz["pb_rd"][ps] else [])
                mk_p = sca(lambda e, o=p, a=PSB[b_s][:, :]: e.activation(
                    o, a, AF.Exp, bias=c12[:, 0:1]),
                    waits=w + [mk_consts], tag=tag + f":e{j}.{hi}")
                evac_done(b_s, mk_p)
                out.append((p, mk_p, ps))
            return out

        pend = [scores(0)]
        if nj > 1:
            pend.append(scores(1))
        for j in range(nj):
            cur = pend.pop(0)
            last = (j == nj - 1)
            for hi, h in enumerate(hs):
                p, mk_p, ps = cur[hi]
                b_o = 4 + hi
                vsl = big[:, OFF_V + j * VSTR + h * VSLOT:
                          OFF_V + j * VSTR + h * VSLOT + 65]
                vw = [mk_p, vev[(j, h // 8)]]
                if h % 2 == 0:
                    pvmk[hi] = accs[hi].add(PSB[b_o][0:65, :], vsl, p, waits=vw,
                                            last=last, region="v", inc=True)
                else:
                    accs[hi].add(PSB[b_o][64:128, :], vsl[:, 0:64], p, waits=vw,
                                 region="v", stop=last)
                    pvmk[hi] = accs[hi].add(PSB[b_o][32:33, :], vsl[:, 64:65], p,
                                            last=last, region="s", inc=True)
                hz["pb_rd"][ps] = sT.mark()
            if j + 2 < nj:
                pend.append(scores(j + 2))
        return [(hs[0], 4, pvmk[0], tag + ":n0"),
                (hs[1], 5, pvmk[1], tag + ":n1")]

    arena_mark = ()
    subfns = [(attn_a, 1), (ffn, 2), (attn_b, 3), (ffn, 4)]
    for fn, sub in subfns[:n_sub]:
        tm = fn(sub, arena_mark)
        arena_mark = (sT.mark(),)

    for ch in range(NCH):
        w = [res[(min(n_sub, 4), c)] for c in range(NCG) if (min(n_sub, 4), c) in res]
        if not w:
            w = [mk_x]
        dma("sync", sdO, yT[ch * 128:(ch + 1) * 128, :],
            x_fm[:, ch * S:(ch + 1) * S], waits=w, tag=f"out{ch}")
    sch.ev("sync", lambda e: e.nop(),
           waits=[(sdO, sdO.planned)]
                 + [(s, s.planned) for s in sdW + sdWc + sdK]
                 + [(sdB, sdB.planned)],
           tag="final")

    nc._tag_map = {}
    sch.simulate()
    with ctx:
        with ExitStack() as sctx:
            for s in ALLSEMS:
                s.handle = sctx.enter_context(nc.semaphore(s.name))
            with nc.Block() as block:
                def emit_for(eng_obj, eng_name):
                    wm = {}
                    for eng, fn, w, incs, tag in sch.events:
                        if eng != eng_name:
                            continue
                        for s, v in w:
                            if s.eng == eng_name and not SIM_EDGES:
                                continue
                            if wm.get(s, -1) < v:
                                eng_obj.wait_ge(s.handle, v)
                                wm[s] = v
                        instr = fn(eng_obj)
                        try:
                            nc._tag_map[instr.ins.name] = (eng_name, tag)
                        except Exception:
                            pass
                        for s, n in incs:
                            instr.then_inc(s.handle, n)

                block.sync(lambda e: emit_for(e, "sync"))
                block.gpsimd(lambda e: emit_for(e, "gpsimd"))
                block.tensor(lambda e: emit_for(e, "tensor"))
                block.vector(lambda e: emit_for(e, "vector"))
                block.scalar(lambda e: emit_for(e, "scalar"))
    return nc


_CACHE = {}


def kernel(**inputs):
    g, per_core = host_layout(inputs)
    if "nc" not in _CACHE:
        _CACHE["nc"] = build_program(4)
    nc = _CACHE["nc"]
    in_maps = []
    for pc in per_core:
        m = dict(g)
        m.update(pc)
        in_maps.append(m)
    r = run_bass_kernel_spmd(nc, in_maps, core_ids=list(range(8)))
    out = np.stack([np.ascontiguousarray(res["yT"].T) for res in r.results])
    return out.astype(np.float32)
